# version 32
# baseline (speedup 1.0000x reference)
import numpy as np
import ml_dtypes
from contextlib import ExitStack

import concourse.bass as bass
import concourse.mybir as mybir
from concourse.bass_utils import run_bass_kernel_spmd

F32 = mybir.dt.float32
BF16 = mybir.dt.bfloat16
AF = mybir.ActivationFunctionType
ALU = mybir.AluOpType
AX = mybir.AxisListType
bfd = ml_dtypes.bfloat16

DECAY = 0.9
NCORE = 8
T_FULL = 2048
D = 1024
DI = 2048
NCH = 16
NSL = 16
E6 = 6
L100 = 100
EPS32 = float(np.finfo(np.float32).eps)

ENGINES = ("sync", "gpsimd", "tensor", "scalar", "vector")


def _host_consts():
    T_np = np.zeros((128, 128), dtype=np.float64)
    for t in range(128):
        T_np[: t + 1, t] = DECAY ** (t - np.arange(t + 1))
    d_vec = DECAY ** (np.arange(128, dtype=np.float64) + 1.0)
    E_np = np.zeros((16, 16 * 128), dtype=np.float64)
    for s in range(16):
        E_np[s, s * 128:(s + 1) * 128] = d_vec
    return {
        "cT": T_np.astype(np.float32),
        "cE": E_np.astype(np.float32),
        "cIb": np.eye(128, dtype=bfd),
        "cIf": np.eye(128, dtype=np.float32),
        "cOb": np.ones((1, 128), dtype=bfd),
        "cOf": np.ones((1, 128), dtype=np.float32),
        "cO16": np.ones((16, 1), dtype=np.float32),
    }


class Sched:

    def __init__(self):
        self.ops = {e: [] for e in ENGINES}
        self.cnt = {}

    def op(self, eng, fn, waits=(), inc=None, n=1):
        self.ops[eng].append((list(waits), fn, inc, n))
        if inc is not None:
            self.cnt[inc] = self.cnt.get(inc, 0) + n
            return self.cnt[inc]
        return None

    def val(self, sem):
        return self.cnt.get(sem, 0)


def build_program(debug_taps=False, nch=NCH, chop=10**9):
    nc = bass.Bass("TRN2", target_bir_lowering=False)

    def din(name, shape, dtype=F32):
        return nc.dram_tensor(name, shape, dtype, kind="ExternalInput").ap()

    x_d = din("x", [T_FULL, D])
    win_d = din("Win", [D, DI])
    bin_d = din("bin_", [1, DI])
    bp_d = din("Bp", [DI, 16])
    cp_d = din("Cp", [16, DI])
    dp_d = din("Dp", [1, DI])
    wout_d = din("Wout", [DI, D])
    bout_d = din("bout", [1, D])
    a_d = din("A", [E6, L100, 16])
    b_d = din("B", [E6, 16, D])
    wg_d = din("w_gate", [D, E6])
    wtw_d = din("wt_w", [D, D])
    wtb_d = din("wt_b", [8, 128])
    wdw_d = din("wd_w", [D, D])
    wdb_d = din("wd_b", [1, D])
    scale_d = din("scale", [1, 1])
    cT_d = din("cT", [128, 128], F32)
    cE_d = din("cE", [16, 2048], F32)
    cIb_d = din("cIb", [128, 128], BF16)
    cIf_d = din("cIf", [128, 128], F32)
    cOb_d = din("cOb", [1, 128], BF16)
    cOf_d = din("cOf", [1, 128], F32)
    cO16_d = din("cO16", [16, 1], F32)
    out_d = nc.dram_tensor("out", [T_FULL, D], F32, kind="ExternalOutput").ap()

    es = ExitStack()

    def sb(name, shape, dtype):
        return es.enter_context(nc.sbuf_tensor(name, shape, dtype))

    with es:
        sc1 = sb("sc1", [128, 2048], F32)
        sc2 = sb("sc2", [128, 2048], F32)
        u_sb = sb("u_sb", [128, 2048], BF16)
        xi_g = sb("xi_g", [128, 2048], BF16)
        xi_dp = sb("xi_dp", [128, 2048], BF16)
        scr4 = sb("scr4", [128, 1024], F32)
        x_buf0 = sb("x_buf0", [128, 1024], F32)
        x_buf1 = sb("x_buf1", [128, 1024], F32)
        out_buf0 = sb("out_buf0", [128, 1024], F32)
        out_buf1 = sb("out_buf1", [128, 1024], F32)
        xn_tm0 = sb("xn_tm0", [128, 1024], BF16)
        xn_tm1 = sb("xn_tm1", [128, 1024], BF16)
        xn_fm0 = sb("xn_fm0", [128, 1024], BF16)
        xn_fm1 = sb("xn_fm1", [128, 1024], BF16)
        expz = sb("expz", [128, 600], BF16)
        w_sb = sb("w_sb", [128, 600], BF16)
        w_t = sb("w_t", [128, 768], BF16)
        sc3 = sb("sc3", [128, 1024], F32)
        Win_sb = sb("Win_sb", [128, 8 * 2048], BF16)
        Wout_sb = sb("Wout_sb", [128, 16 * 1024], BF16)
        wdw_dmaj = sb("wdw_dmaj", [128, 8 * 1024], BF16)
        VW = sb("VW", [128, 6 * 1024], BF16)
        TG = sb("TG", [128, 8 * 606], BF16)
        gb = sb("gb", [128, 2048], BF16)
        dpb = sb("dpb", [128, 2048], BF16)
        T_sb = sb("T_sb", [128, 128], BF16)
        E_sb = sb("E_sb", [16, 2048], BF16)
        Ib_sb = sb("Ib_sb", [128, 128], BF16)
        If_sb = sb("If_sb", [128, 128], F32)
        Ob_sb = sb("Ob_sb", [1, 128], BF16)
        Of_sb = sb("Of_sb", [1, 128], F32)
        O16_sb = sb("O16_sb", [16, 1], F32)
        bin_row = sb("bin_row", [1, 2048], BF16)
        bout_row = sb("bout_row", [1, 1024], BF16)
        wdb_row = sb("wdb_row", [1, 1024], BF16)
        wtb_nat = sb("wtb_nat", [8, 128], BF16)
        wtb_col = sb("wtb_col", [128, 8], BF16)
        wg_nat = sb("wg_nat", [128, 48], BF16)
        bp_nat = sb("bp_nat", [128, 256], F32)
        a_nat = sb("a_nat", [100, 96], BF16)
        scale_sb = sb("scale_sb", [1, 1], F32)
        scale_vec = sb("scale_vec", [128, 1], F32)
        carry_t = sb("carry_t", [16, 128], BF16)
        carry_col = sb("carry_col", [128, 16], BF16)
        stat = sb("stat", [128, 8], F32)
        mxs = sb("mxs", [128, 6], F32)
        m6s = sb("m6s", [128, 6], F32)
        sums = sb("sums", [128, 6], F32)
        m6 = sb("m6", [128, 1], F32)
        eq6 = sb("eq6", [128, 6], F32)
        ec6 = sb("ec6", [128, 6], F32)
        ecz6 = sb("ecz6", [128, 6], F32)
        gates = sb("gates", [128, 6], F32)
        rsum = sb("rsum", [128, 6], F32)
        gsc = sb("gsc", [128, 6], F32)
        tmp6 = sb("tmp6", [128, 6], F32)

        def addr(h):
            return nc.lookup_mloc(h).addr

        def over(name, shape, dtype, host, off=0):
            return es.enter_context(
                _OverGuard(nc.alloc_sbuf_tensor_at(name, shape, dtype,
                                                   offset=addr(host) + off)))

        class _OverGuard:
            def __init__(self, h):
                self.h = h

            def __enter__(self):
                return self.h

            def __exit__(self, *a):
                return False

        assert addr(sc2) == addr(sc1) + 8192, (addr(sc1), addr(sc2))
        assert addr(xi_g) == addr(u_sb) + 4096
        assert addr(xi_dp) == addr(xi_g) + 4096
        assert addr(scr4) == addr(xi_dp) + 4096
        assert addr(x_buf1) == addr(x_buf0) + 4096
        assert addr(out_buf0) == addr(x_buf1) + 4096
        assert addr(xn_tm1) == addr(xn_tm0) + 2048

        Cp_sb = over("Cp_ov", [16, 2048], F32, sc1, 0)
        BpT_sb = over("BpT_ov", [16, 2048], F32, sc1, 8192)
        W2T = over("W2T_ov", [128, 8192], BF16, sc1, 0)
        wtw_nat = over("wtw_ov", [128, 8192], BF16, u_sb, 0)
        B_all = over("Ball_ov", [16, 6144], BF16, x_buf0, 0)
        prod_h = over("prod_ov", [16, 1024], F32, xn_tm0, 0)
        aT_sb = over("aT_ov", [16, 600], BF16, expz, 2048)
        B2_row = over("B2_ov", [1, 1024], BF16, expz, 0)
        Dp_row = over("Dprow_ov", [1, 2048], BF16, xi_dp, 0)
        g_row = over("grow_ov", [1, 2048], BF16, sc3, 0)
        wdw_nat = over("wdwnat_ov", [128, 8192], BF16, sc1, 0)
        wst = over("wst_ov", [128, 8192], F32, Wout_sb, 0)

        ps_big = es.enter_context(nc.psum_tensor("ps_big", [128, 2048], F32))
        ps_tp1 = es.enter_context(nc.psum_tensor("ps_tp1", [128, 1024], BF16))
        ps_tp2 = es.enter_context(nc.psum_tensor("ps_tp2", [128, 1024], BF16))
        ps_zd = es.enter_context(nc.psum_tensor("ps_zd", [128, 1024], F32))

        sem = {}
        for sname in ("dw", "dc", "dx0", "dx1", "so0", "so1", "pe", "ac", "dv",
                      "wdw", "wtw", "win", "wo"):
            sem[sname] = es.enter_context(nc.semaphore("s_" + sname))
        S = Sched()
        M = {}

        DW, DC = sem["dw"], sem["dc"]
        PE, AC, DV = sem["pe"], sem["ac"], sem["dv"]
        WDW, WTW, WIN, WO = sem["wdw"], sem["wtw"], sem["win"], sem["wo"]

        const_loads = [
            (Ib_sb, cIb_d), (If_sb, cIf_d),
            (Ob_sb, cOb_d), (Of_sb, cOf_d), (O16_sb, cO16_d),
            (bp_nat, bp_d.rearrange("(s p) j -> p s j", p=128)),
            (Cp_sb, cp_d), (scale_sb, scale_d),
        ]
        for dst, src in const_loads:
            S.op("sync", (lambda e, d=dst, s=src: e.dma_start(out=d[:, :] if d.shape[0] != 16 or True else d, in_=s)),
                 inc=DC, n=16)
        M["dc_all"] = 16 * len(const_loads)

        gq = []
        ndw = [0]

        def gdma(dst_ap, src_ap, waits=(), gsem=None):
            gsem = DW if gsem is None else gsem
            v = S.op("gpsimd", (lambda e, d=dst_ap, s=src_ap: e.dma_start(out=d, in_=s)),
                     waits=waits, inc=gsem, n=16)
            ndw[0] += 1
            return v

        M["dw_smalls"] = None
        gdma(T_sb[:, :], cT_d)
        gdma(E_sb[:, :], cE_d)
        gdma(bin_row[:, :], bin_d)
        gdma(bout_row[:, :], bout_d)
        gdma(wdb_row[:, :], wdb_d)
        gdma(Dp_row[:, :], dp_d)
        gdma(wtb_nat[:, :], wtb_d)
        gdma(wg_nat[:, :].rearrange("p (k e) -> p k e", k=8),
             wg_d.rearrange("(k p) e -> p k e", p=128))
        gdma(a_nat[:, :].rearrange("p (e r) -> p e r", e=6),
             a_d.rearrange("e l r -> l e r"))
        M["dw_smalls"] = gdma(B_all[:, :].rearrange("r (e c) -> r e c", e=6),
                              b_d.rearrange("e r c -> r e c"))
        M["dw_wdw"] = None

        def f_bptp(e):
            r = None
            for s in range(16):
                r = e.transpose(ps_big[0:16, s * 128:(s + 1) * 128],
                                bp_nat[:, s * 16:(s + 1) * 16], If_sb[:, :])
            return r
        M["pe_bptp"] = S.op("tensor", f_bptp, waits=[(DC, M["dc_all"])], inc=PE)

        M["ac_bpT"] = S.op("scalar",
                           lambda e: e.activation(BpT_sb[:, :], ps_big[0:16, :], AF.Identity, bias=0.0),
                           waits=[(PE, M["pe_bptp"])], inc=AC)

        def f_prod(e, h):
            return e.tensor_tensor(prod_h[:, :], BpT_sb[:, h * 1024:(h + 1) * 1024],
                                   Cp_sb[:, h * 1024:(h + 1) * 1024], ALU.mult)
        def f_grow(e, h):
            r = None
            for n in range(2):
                r = e.matmul(ps_big[0:1, (h * 2 + n) * 512:(h * 2 + n + 1) * 512],
                             O16_sb[:, :], prod_h[0:16, n * 512:(n + 1) * 512],
                             start=True, stop=True)
            return r
        M["dv_prod0"] = S.op("vector", lambda e: f_prod(e, 0),
                             waits=[(AC, M["ac_bpT"]), (DC, M["dc_all"])], inc=DV)
        M["pe_grow0"] = S.op("tensor", lambda e: f_grow(e, 0),
                             waits=[(DV, M["dv_prod0"])], inc=PE)
        M["dv_prod1"] = S.op("vector", lambda e: f_prod(e, 1),
                             waits=[(PE, M["pe_grow0"])], inc=DV)
        M["pe_grow1"] = S.op("tensor", lambda e: f_grow(e, 1),
                             waits=[(DV, M["dv_prod1"])], inc=PE)
        M["ac_grow"] = S.op("scalar",
                            lambda e: e.activation(g_row[:, :], ps_big[0:1, :], AF.Copy, bias=0.0),
                            waits=[(PE, M["pe_grow1"])], inc=AC)

        def f_bcast(e, row, n0):
            return e.matmul(ps_big[:, n0 * 512:(n0 + 1) * 512], Ob_sb[:, :],
                            row[0:1, n0 * 512:(n0 + 1) * 512], start=True, stop=True)
        def f_bc_all(e, row):
            r = None
            for n in range(4):
                r = f_bcast(e, row, n)
            return r
        M["pe_gbc"] = S.op("tensor", lambda e: f_bc_all(e, g_row),
                           waits=[(AC, M["ac_grow"])], inc=PE)
        M["ac_gb"] = S.op("scalar",
                          lambda e: e.activation(gb[:, :], ps_big[:, :], AF.Copy, bias=0.0),
                          waits=[(PE, M["pe_gbc"])], inc=AC)
        M["pe_dbc"] = S.op("tensor", lambda e: f_bc_all(e, Dp_row),
                           waits=[(AC, M["ac_gb"]), (DW, M["dw_smalls"])], inc=PE)
        M["ac_dpb"] = S.op("scalar",
                           lambda e: e.activation(dpb[:, :], ps_big[:, :], AF.Copy, bias=0.0),
                           waits=[(PE, M["pe_dbc"])], inc=AC)
        M["pe_sbc"] = S.op("tensor",
                           lambda e: e.matmul(ps_big[:, 0:1], Of_sb[:, :], scale_sb[:, :],
                                              start=True, stop=True),
                           waits=[(AC, M["ac_dpb"])], inc=PE)
        M["ac_svec"] = S.op("scalar",
                            lambda e: e.activation(scale_vec[:, :], ps_big[:, 0:1], AF.Identity, bias=0.0),
                            waits=[(PE, M["pe_sbc"])], inc=AC)

        M["pe_wtbtp"] = S.op("tensor",
                             lambda e: e.transpose(ps_tp1[:, 0:8], wtb_nat[0:8, :], Ib_sb[0:8, 0:8]),
                             waits=[(DW, M["dw_smalls"]), (DC, M["dc_all"])], inc=PE)
        M["dv_wtb"] = S.op("vector",
                           lambda e: e.tensor_copy(wtb_col[:, :], ps_tp1[:, 0:8]),
                           waits=[(PE, M["pe_wtbtp"])], inc=DV)

        def f_atp(e):
            r = None
            for ex in range(6):
                r = e.transpose(ps_tp2[0:16, ex * 100:(ex + 1) * 100],
                                a_nat[0:100, ex * 16:(ex + 1) * 16], Ib_sb[0:100, 0:100])
            return r
        M["pe_atp"] = S.op("tensor", f_atp, waits=[(DW, M["dw_smalls"])], inc=PE)
        M["ac_aT"] = S.op("scalar",
                          lambda e: e.activation(aT_sb[:, :], ps_tp2[0:16, 0:600], AF.Copy, bias=0.0),
                          waits=[(PE, M["pe_atp"])], inc=AC)

        for kc in range(8):
            def f_tokmm(e, kc=kc):
                r = None
                for ex in range(6):
                    slot = ex * 100 if ex < 5 else 512
                    r = e.matmul(ps_zd[:, slot:slot + 100],
                                 B_all[0:16, ex * 1024 + kc * 128: ex * 1024 + (kc + 1) * 128],
                                 aT_sb[0:16, ex * 100:(ex + 1) * 100],
                                 start=True, stop=True)
                return r
            wl = [(AC, M["ac_aT"]), (DW, M["dw_smalls"])]
            if kc > 0:
                wl.append((DV, M[f"dv_tok{kc-1}"]))
            M[f"pe_tok{kc}"] = S.op("tensor", f_tokmm, waits=wl, inc=PE)

            def f_tokev(e, kc=kc):
                e.tensor_copy(TG[:, kc * 606: kc * 606 + 500], ps_zd[:, 0:500])
                return e.tensor_copy(TG[:, kc * 606 + 500: kc * 606 + 600], ps_zd[:, 512:612])
            M[f"dv_tok{kc}"] = S.op("vector", f_tokev,
                                    waits=[(PE, M[f"pe_tok{kc}"])], inc=DV)
        M["pe_tokdone"] = M["pe_tok7"]
        M["dv_tokdone"] = M["dv_tok7"]

        M["dv_wg"] = S.op("vector",
                          lambda e: e.tensor_copy(
                              TG[:, :].rearrange("p (k c) -> p k c", k=8)[:, :, 600:606],
                              wg_nat[:, :].rearrange("p (k e) -> p k e", k=8)),
                          waits=[(DW, M["dw_smalls"]), (DV, M["dv_tokdone"])], inc=DV)

        S.op("sync", lambda e: e.dma_start(
                 out=wst[:, :].rearrange("p (k d) -> p k d", k=8),
                 in_=wdw_d.rearrange("(k p) d -> p k d", p=128)),
             inc=WDW, n=16)
        def f_wdwc(e):
            e.activation(wdw_nat[:, 0:4096], wst[:, 0:4096], AF.Copy, bias=0.0)
            return e.activation(wdw_nat[:, 4096:8192], wst[:, 4096:8192],
                                AF.Copy, bias=0.0)
        M["ac_wdwc"] = S.op("scalar", f_wdwc,
                            waits=[(WDW, 16), (DV, M["dv_prod1"])],
                            inc=AC)
        S.op("sync", lambda e: e.dma_start(
                 out=wst[:, :].rearrange("p (k d) -> p k d", k=8),
                 in_=wtw_d.rearrange("(k p) d -> p k d", p=128)),
             waits=[(AC, M["ac_wdwc"])], inc=WTW, n=16)
        def f_wtwc(e):
            e.activation(wtw_nat[:, 0:4096], wst[:, 0:4096], AF.Copy, bias=0.0)
            return e.activation(wtw_nat[:, 4096:8192], wst[:, 4096:8192],
                                AF.Copy, bias=0.0)
        M["ac_wtwc"] = S.op("scalar", f_wtwc,
                            waits=[(WTW, 16), (PE, M["pe_dbc"])],
                            inc=AC)
        for k in range(8):
            gdma(Win_sb[:, k * 2048:(k + 1) * 2048],
                 win_d[k * 128:(k + 1) * 128, :], gsem=WIN)
        M["dw_win"] = S.val(WIN)
        for k in range(16):
            gdma(Wout_sb[:, k * 1024:(k + 1) * 1024],
                 wout_d[k * 128:(k + 1) * 128, :],
                 waits=([(AC, M["ac_wtwc"])] if k == 0 else ()), gsem=WO)
        M["dw_wout"] = S.val(WO)

        for rj in range(8):
            def f_wdtp(e, rj=rj):
                r = None
                for k in range(8):
                    r = e.transpose(ps_tp1[:, k * 128:(k + 1) * 128],
                                    wdw_nat[:, k * 1024 + rj * 128: k * 1024 + (rj + 1) * 128],
                                    Ib_sb[:, :])
                return r
            wl = [(AC, M["ac_wdwc"]), (DV, M["dv_wtb"])]
            if rj > 0:
                wl.append((DV, M[f"dv_wdtp{rj-1}"]))
            M[f"pe_wdtp{rj}"] = S.op("tensor", f_wdtp, waits=wl, inc=PE)
            M[f"dv_wdtp{rj}"] = S.op("vector",
                                     lambda e, rj=rj: e.tensor_copy(
                                         wdw_dmaj[:, rj * 1024:(rj + 1) * 1024], ps_tp1[:, :]),
                                     waits=[(PE, M[f"pe_wdtp{rj}"])], inc=DV)
        M["pe_wdtp_done"] = M["pe_wdtp7"]
        M["dv_wdtp_done"] = M["dv_wdtp7"]

        w2t_seq = [(kcc, n) for kcc in range(8) for n in range(2)]
        for idx, (kcc, n) in enumerate(w2t_seq):
            bank = idx % 4

            def f_w2t(e, kcc=kcc, n=n, bank=bank):
                r = None
                for kd in range(8):
                    r = e.matmul(ps_big[:, bank * 512:(bank + 1) * 512],
                                 wtw_nat[:, kd * 1024 + kcc * 128: kd * 1024 + (kcc + 1) * 128],
                                 wdw_dmaj[:, kd * 1024 + n * 512: kd * 1024 + (n + 1) * 512],
                                 start=(kd == 0), stop=(kd == 7))
                return r
            wl = [(AC, M["ac_wtwc"]), (DV, M["dv_wdtp_done"]), (AC, M["ac_svec"])]
            if idx >= 4:
                wl.append((DV, M[f"dv_w2t{idx-4}"]))
            M[f"pe_w2t{idx}"] = S.op("tensor", f_w2t, waits=wl, inc=PE)
            M[f"dv_w2t{idx}"] = S.op("vector",
                                     lambda e, kcc=kcc, n=n, bank=bank: e.tensor_copy(
                                         W2T[:, kcc * 1024 + n * 512: kcc * 1024 + (n + 1) * 512],
                                         ps_big[:, bank * 512:(bank + 1) * 512]),
                                     waits=[(PE, M[f"pe_w2t{idx}"])], inc=DV)
        M["pe_w2t_done"] = M["pe_w2t15"]
        M["dv_w2t_done"] = M["dv_w2t15"]

        vw_seq = [(ex, n) for ex in range(6) for n in range(2)]
        for idx, (ex, n) in enumerate(vw_seq):
            bank = idx % 2

            def f_vw(e, ex=ex, n=n, bank=bank):
                r = None
                for kcc in range(8):
                    r = e.matmul(ps_zd[0:100, bank * 512:(bank + 1) * 512],
                                 TG[:, kcc * 606 + ex * 100: kcc * 606 + (ex + 1) * 100],
                                 W2T[:, kcc * 1024 + n * 512: kcc * 1024 + (n + 1) * 512],
                                 start=(kcc == 0), stop=(kcc == 7))
                return r
            wl = [(DV, M["dv_w2t_done"]), (DV, M["dv_tokdone"])]
            if idx >= 2:
                wl.append((AC, M[f"ac_vw{idx-2}"]))
            M[f"pe_vw{idx}"] = S.op("tensor", f_vw, waits=wl, inc=PE)
            M[f"ac_vw{idx}"] = S.op("scalar",
                                    lambda e, ex=ex, n=n, bank=bank: e.activation(
                                        VW[0:100, ex * 1024 + n * 512: ex * 1024 + (n + 1) * 512],
                                        ps_zd[0:100, bank * 512:(bank + 1) * 512],
                                        AF.Copy, bias=0.0),
                                    waits=[(PE, M[f"pe_vw{idx}"])], inc=AC)
        M["pe_vw_done"] = M["pe_vw11"]
        M["ac_vw_done"] = M["ac_vw11"]

        def f_b2(e):
            r = None
            for n in range(2):
                for kd in range(8):
                    r = e.matmul(ps_zd[0:1, n * 512:(n + 1) * 512],
                                 wtb_col[:, kd:kd + 1],
                                 wdw_dmaj[:, kd * 1024 + n * 512: kd * 1024 + (n + 1) * 512],
                                 start=(kd == 0), stop=(kd == 7))
            return r
        M["pe_b2"] = S.op("tensor", f_b2,
                          waits=[(AC, M["ac_vw_done"]), (DV, M["dv_wtb"])], inc=PE)
        M["ac_b2"] = S.op("scalar",
                          lambda e: e.activation(B2_row[:, :], ps_zd[0:1, :], AF.Copy, bias=0.0),
                          waits=[(PE, M["pe_b2"])], inc=AC)

        def f_b2vw(e):
            r = None
            for ex in range(6):
                r = e.activation(VW[0:1, ex * 1024:(ex + 1) * 1024], B2_row[:, :],
                                 AF.Copy, bias=0.0)
            return r
        M["ac_b2vw"] = S.op("scalar", f_b2vw, waits=[(AC, M["ac_b2"])], inc=AC)

        M["dv_ct0"] = S.op("vector", lambda e: e.memset(carry_t[:, :], 0.0), inc=DV)

        DX = [sem["dx0"], sem["dx1"]]
        SO = [sem["so0"], sem["so1"]]
        nx = [0, 0]
        nso = [0, 0]

        def x_load(c, waits):
            p = c % 2
            buf = x_buf0 if p == 0 else x_buf1
            S.op("sync", (lambda e, b=buf, c=c: e.dma_start(out=b[:, :],
                                                            in_=x_d[c * 128:(c + 1) * 128, :])),
                 waits=waits, inc=DX[p], n=16)
            nx[p] += 1
            return 16 * nx[p]

        M["x0"] = x_load(0, [(PE, M["pe_tokdone"])])
        M["x1"] = x_load(1, [(PE, M["pe_tokdone"])])

        v_final = {}
        _chopn = [0]
        _trunc = [False]

        def cop(*a, **kw):
            _chopn[0] += 1
            if _chopn[0] > chop:
                _trunc[0] = True
                return None
            return S.op(*a, **kw)
        def _bufs(c):
            p = c % 2
            return p, (x_buf0 if p == 0 else x_buf1), (out_buf0 if p == 0 else out_buf1)

        def _xn(c):
            p = c % 2
            return (xn_tm0 if p == 0 else xn_tm1), (xn_fm0 if p == 0 else xn_fm1)

        def emit_front(c):
            p, xb, ob = _bufs(c)

            M[f"ac_sq{c}"] = cop("scalar",
                                  lambda e, xb=xb: e.activation(scr4[:, :], xb[:, :], AF.Square),
                                  waits=[(DX[p], 16 * (c // 2 + 1))] +
                                        ([(PE, M["pe_w2t_done"])] if c == 0 else []) +
                                        ([(DV, v_final[c - 2])] if c >= 2 else []),
                                  inc=AC)

            def f_stats(e, xb=xb):
                e.reduce_sum(stat[:, 0:1], xb[:, :], axis=AX.X)
                return e.reduce_sum(stat[:, 1:2], scr4[:, :], axis=AX.X)
            M[f"dv_red{c}"] = cop("vector", f_stats,
                                   waits=[(AC, M[f"ac_sq{c}"]),
                                          (DX[p], 16 * (c // 2 + 1))], inc=DV)

            def f_m1(e):
                e.tensor_scalar(stat[:, 2:3], stat[:, 0:1], 1.0 / 1024.0, None, op0=ALU.mult)
                return e.tensor_scalar(stat[:, 3:4], stat[:, 1:2], 1.0 / 1024.0, None, op0=ALU.mult)
            M[f"dv_mv1{c}"] = cop("vector", f_m1, waits=[(DV, M[f"dv_red{c}"])], inc=DV)
            M[f"dv_mv2{c}"] = cop("vector",
                                  lambda e: e.tensor_tensor(stat[:, 4:5], stat[:, 2:3],
                                                            stat[:, 2:3], ALU.mult),
                                  waits=[(DV, M[f"dv_mv1{c}"])], inc=DV)
            M[f"dv_mv{c}"] = cop("vector",
                                 lambda e: e.tensor_tensor(stat[:, 3:4], stat[:, 3:4],
                                                           stat[:, 4:5], ALU.subtract),
                                 waits=[(DV, M[f"dv_mv2{c}"])], inc=DV)

            M[f"ac_lnv{c}"] = cop("scalar",
                                  lambda e: e.activation(stat[:, 5:6], stat[:, 3:4], AF.Ln,
                                                         bias=np.float32(1e-5)),
                                  waits=[(DV, M[f"dv_mv{c}"])], inc=AC)
            M[f"ac_rstd{c}"] = cop("scalar",
                                   lambda e: e.activation(stat[:, 6:7], stat[:, 5:6], AF.Exp,
                                                          scale=np.float32(-0.5)),
                                   waits=[(AC, M[f"ac_lnv{c}"])], inc=AC)

            xntm, xnfm = _xn(c)
            M[f"dv_xn{c}"] = cop("vector",
                                  (lambda e, xb=xb, xntm=xntm: e.tensor_scalar(
                                      xntm[:, :], xb[:, :], stat[:, 2:3], stat[:, 6:7],
                                      op0=ALU.subtract, op1=ALU.mult)),
                                  waits=[(AC, M[f"ac_rstd{c}"])] +
                                        ([(PE, M["pe_grow1"])] if c == 0 else []),
                                  inc=DV)

        def emit_body(c):
            p, xb, ob = _bufs(c)
            xn_tm, xn_fm = _xn(c)
            for h in range(2):
                def f_m1(e, h=h):
                    r = None
                    for kc in range(8):
                        lhs = xn_fm[:, kc * 128:(kc + 1) * 128]
                        for n in range(2 * h, 2 * h + 2):
                            e.matmul(ps_big[:, n * 512:(n + 1) * 512], lhs,
                                     Win_sb[:, kc * 2048 + n * 512: kc * 2048 + (n + 1) * 512],
                                     start=(kc == 0), stop=False)
                    for n in range(2 * h, 2 * h + 2):
                        r = e.matmul(ps_big[:, n * 512:(n + 1) * 512], Ob_sb[:, :],
                                     bin_row[0:1, n * 512:(n + 1) * 512],
                                     start=False, stop=True)
                    return r
                wl = [(DV, M[f"dv_xnfm{c}"]), (WIN, M["dw_win"])]
                if h == 0:
                    if c > 0:
                        wl.append((AC, M[f"ac_yev{c-1}"]))
                    else:
                        wl.append((DV, M["dv_w2t_done"]))
                M[f"pe_m1{c}_{h}"] = cop("tensor", f_m1, waits=wl, inc=PE)
            M[f"pe_m1{c}"] = M[f"pe_m1{c}_1"]

            def f_z(e):
                r = None
                for kc in range(8):
                    lhs = xn_fm[:, kc * 128:(kc + 1) * 128]
                    e.matmul(ps_zd[:, 0:300], lhs,
                             TG[:, kc * 606: kc * 606 + 300],
                             start=(kc == 0), stop=(kc == 7))
                    r = e.matmul(ps_zd[:, 512:818], lhs,
                                 TG[:, kc * 606 + 300: kc * 606 + 606],
                                 start=(kc == 0), stop=(kc == 7))
                return r
            wl = [(DV, M[f"dv_xnfm{c}"])]
            if c > 0:
                wl.append((DV, v_final[c - 1]))
            else:
                wl.append((AC, M["ac_b2vw"]))
                wl.append((DV, M["dv_wg"]))
            M[f"pe_z{c}"] = cop("tensor", f_z, waits=wl, inc=PE)

            def emit_silu(h):
                H = slice(h * 1024, (h + 1) * 1024)
                wl = [(PE, M[f"pe_m1{c}_{h}"])]
                if c == 0:
                    wl.append((DV, M["dv_w2t_done"]))
                elif h == 0:
                    wl.append((DV, M[f"dv_fa{c-1}"]))
                M[f"ac_sigA{c}_{h}"] = cop("scalar",
                                      (lambda e, H=H: e.activation(sc1[:, H], ps_big[:, H],
                                                                   AF.Exp, scale=np.float32(-1.0))),
                                      waits=wl, inc=AC)
                M[f"ac_sigB{c}_{h}"] = cop("scalar",
                                      (lambda e, H=H: e.activation(sc2[:, H], sc1[:, H],
                                                                   AF.Ln, bias=np.float32(1.0))),
                                      waits=[(AC, M[f"ac_sigA{c}_{h}"])], inc=AC)
                M[f"ac_sig{c}_{h}"] = cop("scalar",
                                     (lambda e, H=H: e.activation(sc1[:, H], sc2[:, H],
                                                                  AF.Exp, scale=np.float32(-1.0))),
                                     waits=[(AC, M[f"ac_sigB{c}_{h}"])], inc=AC)
                wl = [(AC, M[f"ac_sig{c}_{h}"])]
                if c == 0 and h == 0:
                    wl.append((PE, M["pe_w2t_done"]))
                M[f"dv_u{c}_{h}"] = cop("vector",
                                    (lambda e, H=H: e.tensor_tensor(u_sb[:, H], ps_big[:, H],
                                                                    sc1[:, H], ALU.mult)),
                                    waits=wl, inc=DV)
                M[f"dv_presg{c}_{h}"] = cop("vector",
                                    (lambda e, H=H: e.tensor_tensor(xi_g[:, H], u_sb[:, H],
                                                                    gb[:, H], ALU.mult)),
                                    waits=[(DV, M[f"dv_u{c}_{h}"])] +
                                          ([(AC, M["ac_dpb"])] if c == 0 and h == 0 else []),
                                    inc=DV)

            emit_silu(0)

            M[f"dv_m6{c}"] = cop("vector",
                                 lambda e: e.reduce_max(m6[:, :], ps_zd[:, 812:818], axis=AX.X),
                                 waits=[(PE, M[f"pe_z{c}"])], inc=DV)
            prev = M[f"dv_m6{c}"]
            def st1(e):
                e.tensor_scalar(eq6[:, :], ps_zd[:, 812:818], m6[:, 0:1], None, op0=ALU.is_equal)
                return e.memset(ec6[:, 0:1], 0.0)
            prev = cop("vector", st1, waits=[(DV, prev)], inc=DV)
            for ex in range(1, 6):
                prev = cop("vector",
                           (lambda e, ex=ex: e.tensor_tensor(ec6[:, ex:ex + 1], ec6[:, ex - 1:ex],
                                                             eq6[:, ex - 1:ex], ALU.add)),
                           waits=[(DV, prev)], inc=DV)
            prev = cop("vector",
                       lambda e: e.tensor_scalar(ecz6[:, :], ec6[:, :], 0.0, None, op0=ALU.is_equal),
                       waits=[(DV, prev)], inc=DV)
            M[f"dv_gate6{c}"] = cop("vector",
                       lambda e: e.tensor_tensor(gates[:, :], eq6[:, :], ecz6[:, :], ALU.mult),
                       waits=[(DV, prev)], inc=DV)

            emit_silu(1)
            M[f"ac_sig{c}"] = M[f"ac_sig{c}_1"]

            def f_exps(e):
                r = None
                for ex in range(6):
                    off = ex * 100 if ex < 3 else 512 + (ex - 3) * 100
                    r = e.activation(expz[:, ex * 100:(ex + 1) * 100],
                                     ps_zd[:, off:off + 100], AF.Exp,
                                     scale=np.float32(1.0 / 32.0),
                                     accum_out=sums[:, ex:ex + 1])
                return r
            wl = [(PE, M[f"pe_z{c}"])]
            if c == 0:
                wl.append((AC, M["ac_aT"]))
                wl.append((AC, M["ac_b2vw"]))
            M[f"ac_exps{c}"] = cop("scalar", f_exps, waits=wl, inc=AC)


            prev = cop("vector",
                       lambda e: e.reciprocal(rsum[:, :], sums[:, :]),
                       waits=[(DV, M[f"dv_gate6{c}"]), (AC, M[f"ac_exps{c}"])], inc=DV)
            M[f"dv_gsc{c}"] = cop("vector",
                                  lambda e: e.tensor_tensor(gsc[:, :], gates[:, :], rsum[:, :], ALU.mult),
                                  waits=[(DV, prev)], inc=DV)

            def f_wbuild(e):
                r = None
                for ex in range(6):
                    r = e.tensor_scalar(w_sb[:, ex * 100:(ex + 1) * 100],
                                        expz[:, ex * 100:(ex + 1) * 100],
                                        gsc[:, ex:ex + 1], None, op0=ALU.mult)
                return r
            M[f"dv_w0{c}"] = cop("vector", f_wbuild, waits=[(DV, M[f"dv_gsc{c}"])], inc=DV)
            M[f"dv_w1{c}"] = cop("vector",
                                 lambda e: e.tensor_tensor(tmp6[:, :], gates[:, :],
                                                           w_sb[:, 0:600:100], ALU.subtract),
                                 waits=[(DV, M[f"dv_w0{c}"])], inc=DV)
            M[f"dv_w{c}"] = cop("vector",
                                lambda e: e.tensor_copy(w_sb[:, 0:600:100], tmp6[:, :]),
                                waits=[(DV, M[f"dv_w1{c}"])], inc=DV)

            for h in range(2):
                H = slice(h * 1024, (h + 1) * 1024)
                wl = [(DV, M[f"dv_u{c}_{h}"])]
                if c == 0 and h == 0:
                    wl.append((AC, M["ac_dpb"]))
                M[f"dv_pres{c}_{h}"] = cop("vector",
                                    (lambda e, H=H: e.tensor_tensor(xi_dp[:, H], u_sb[:, H],
                                                                    dpb[:, H], ALU.mult)),
                                    waits=wl, inc=DV)
            M[f"dv_pres{c}"] = M[f"dv_pres{c}_1"]

            for h in range(2):
                def f_corr(e, c=c, h=h):
                    r = None
                    for b in range(2 * h, 2 * h + 2):
                        r = e.matmul(ps_big[:, b * 512:(b + 1) * 512], carry_t[:, :],
                                     E_sb[:, b * 512:(b + 1) * 512], start=True, stop=False)
                    return r
                wl = [(DV, M[f"dv_u{c}_{h}"])]
                if h == 0:
                    if c > 0:
                        wl.append((AC, M[f"ac_ct{c-1}"]))
                    else:
                        wl.append((DV, M["dv_ct0"]))
                M[f"pe_corr{c}_{h}"] = cop("tensor", f_corr, waits=wl, inc=PE)
                def f_tmm(e, c=c, h=h):
                    r = None
                    for s in range(8 * h, 8 * h + 8):
                        r = e.matmul(ps_big[:, s * 128:(s + 1) * 128],
                                     xi_g[:, s * 128:(s + 1) * 128], T_sb[:, :],
                                     start=False, stop=(s % 4 == 3))
                    return r
                M[f"pe_scan{c}_{h}"] = cop("tensor", f_tmm,
                                           waits=[(DV, M[f"dv_presg{c}_{h}"])], inc=PE)
            M[f"pe_scan{c}"] = M[f"pe_scan{c}_1"]

            M[f"ac_ext{c}"] = cop("scalar",
                                   lambda e: e.activation(carry_col[:, :], ps_big[:, 127:2048:128],
                                                          AF.Copy, bias=0.0),
                                   waits=[(PE, M[f"pe_scan{c}"])], inc=AC)

            M[f"pe_ctp{c}"] = cop("tensor",
                                   lambda e: e.transpose(ps_tp1[0:16, 896:1024],
                                                         carry_col[:, :], Ib_sb[:, :]),
                                   waits=[(AC, M[f"ac_ext{c}"])], inc=PE)
            M[f"ac_ct{c}"] = cop("scalar",
                                  lambda e: e.activation(carry_t[:, :], ps_tp1[0:16, 896:1024],
                                                         AF.Copy, bias=0.0),
                                  waits=[(PE, M[f"pe_ctp{c}"])], inc=AC)

            def f_wtp(e):
                r = None
                for ex in range(6):
                    r = e.transpose(ps_tp2[0:100, ex * 128:(ex + 1) * 128],
                                    w_sb[:, ex * 100:(ex + 1) * 100], Ib_sb[:, :])
                return r
            wl = [(DV, M[f"dv_w{c}"])]
            if c > 0:
                wl.append((AC, M[f"ac_wt{c-1}"]))
            else:
                wl.append((AC, M["ac_aT"]))
            M[f"pe_wtp{c}"] = cop("tensor", f_wtp, waits=wl, inc=PE)

            M[f"ac_wt{c}"] = cop("scalar",
                                  lambda e: e.activation(w_t[0:100, :], ps_tp2[0:100, 0:768],
                                                         AF.Copy, bias=0.0),
                                  waits=[(PE, M[f"pe_wtp{c}"])], inc=AC)

            def f_imm(e):
                r = None
                for s in range(NSL):
                    r = e.matmul(ps_big[:, s * 128:(s + 1) * 128],
                                 xi_dp[:, s * 128:(s + 1) * 128], Ib_sb[:, :],
                                 start=False, stop=True, skip_group_check=True)
                return r
            M[f"pe_imm{c}"] = cop("tensor", f_imm,
                                   waits=[(AC, M[f"ac_ext{c}"]), (DV, M[f"dv_pres{c}"])], inc=PE)

            M[f"ac_yev{c}"] = cop("scalar",
                                   lambda e: e.activation(u_sb[:, :], ps_big[:, :],
                                                          AF.Copy, bias=0.0),
                                   waits=[(PE, M[f"pe_imm{c}"]), (DV, M[f"dv_pres{c}"])], inc=AC)


        def emit_body_c(c):
            p, xb, ob = _bufs(c)
            xn_tm, xn_fm = _xn(c)
            def f_delta(e):
                r = None
                for kc in range(8):
                    lhs = xn_fm[:, kc * 128:(kc + 1) * 128]
                    for n in range(2):
                        r = e.matmul(ps_zd[:, n * 512:(n + 1) * 512], lhs,
                                     wdw_dmaj[:, kc * 1024 + n * 512: kc * 1024 + (n + 1) * 512],
                                     start=(kc == 0), stop=False)
                for ex in range(6):
                    lhs = w_t[0:100, ex * 128:(ex + 1) * 128]
                    for n in range(2):
                        r = e.matmul(ps_zd[:, n * 512:(n + 1) * 512], lhs,
                                     VW[0:100, ex * 1024 + n * 512: ex * 1024 + (n + 1) * 512],
                                     start=False, stop=False)
                for n in range(2):
                    r = e.matmul(ps_zd[:, n * 512:(n + 1) * 512], Ob_sb[:, :],
                                 wdb_row[0:1, n * 512:(n + 1) * 512],
                                 start=False, stop=True)
                return r
            wl = [(AC, M[f"ac_wt{c}"]), (AC, M[f"ac_exps{c}"]), (DV, M[f"dv_gsc{c}"])]
            if c == 0:
                wl.append((AC, M["ac_b2vw"]))
            M[f"pe_delta{c}"] = cop("tensor", f_delta, waits=wl, inc=PE)

            M[f"ac_cb{c}"] = cop("scalar",
                                  lambda e: e.activation(sc3[:, :], ps_zd[:, :], AF.Exp,
                                                         scale=scale_vec[:, 0:1]),
                                  waits=[(PE, M[f"pe_delta{c}"]), (AC, M["ac_svec"])], inc=AC)

            M[f"dv_epsA{c}"] = cop("vector",
                                   lambda e: e.tensor_scalar(sc2[:, 1024:2048], sc3[:, :], 0.0,
                                                             EPS32, op0=ALU.is_equal, op1=ALU.mult),
                                   waits=[(AC, M[f"ac_cb{c}"])], inc=DV)
            M[f"dv_eps{c}"] = cop("vector",
                                  lambda e: e.tensor_tensor(sc2[:, 0:1024], sc3[:, :],
                                                            sc2[:, 1024:2048], ALU.add),
                                  waits=[(DV, M[f"dv_epsA{c}"])], inc=DV)

            M[f"ac_ln{c}"] = cop("scalar",
                                  lambda e: e.activation(sc1[:, 0:1024], sc2[:, 0:1024], AF.Ln),
                                  waits=[(DV, M[f"dv_eps{c}"])], inc=AC)

            M[f"dv_ps{c}"] = cop("vector",
                                  lambda e: e.tensor_tensor(sc2[:, 0:1024], sc1[:, 0:1024],
                                                            xn_tm[:, :], ALU.add),
                                  waits=[(AC, M[f"ac_ln{c}"])], inc=DV)


        def emit_front_b(c):
            p, xb, ob = _bufs(c)
            xn_tm, xn_fm = _xn(c)
            def f_xntp(e):
                r = None
                for k in range(8):
                    r = e.transpose(ps_tp1[:, k * 128:(k + 1) * 128],
                                    xn_tm[:, k * 128:(k + 1) * 128], Ib_sb[:, :])
                return r
            wl = [(DV, M[f"dv_xn{c}"])]
            if c > 0:
                wl.append((DV, M[f"dv_xnfm{c-1}"]))
                wl.append((AC, M[f"ac_ct{c-1}"]))
            else:
                wl.append((DV, M["dv_wdtp_done"]))
            M[f"pe_xntp{c}"] = cop("tensor", f_xntp, waits=wl, inc=PE)

            M[f"dv_xnfm{c}"] = cop("vector",
                                    lambda e: e.tensor_copy(xn_fm[:, :], ps_tp1[:, :]),
                                    waits=[(PE, M[f"pe_xntp{c}"])], inc=DV)



        def emit_tail_a(c):
            p, xb, ob = _bufs(c)
            xn_tm, xn_fm = _xn(c)
            M[f"ac_sgA{c}"] = cop("scalar",
                                  lambda e: e.activation(sc3[:, :], sc2[:, 0:1024], AF.Exp,
                                                         scale=np.float32(-1.0)),
                                  waits=[(DV, M[f"dv_ps{c}"])], inc=AC)
            M[f"ac_sgB{c}"] = cop("scalar",
                                  lambda e: e.activation(sc1[:, 1024:2048], sc3[:, :], AF.Ln,
                                                         bias=np.float32(1.0)),
                                  waits=[(AC, M[f"ac_sgA{c}"])], inc=AC)
            M[f"ac_sg{c}"] = cop("scalar",
                                 lambda e: e.activation(sc3[:, :], sc1[:, 1024:2048], AF.Exp,
                                                        scale=np.float32(-1.0)),
                                 waits=[(AC, M[f"ac_sgB{c}"])], inc=AC)

            def f_m3(e):
                r = None
                for k in range(16):
                    lhs = u_sb[:, k * 128:(k + 1) * 128]
                    for n in range(2):
                        r = e.matmul(ps_zd[:, n * 512:(n + 1) * 512], lhs,
                                     Wout_sb[:, k * 1024 + n * 512: k * 1024 + (n + 1) * 512],
                                     start=(k == 0), stop=False)
                for n in range(2):
                    r = e.matmul(ps_zd[:, n * 512:(n + 1) * 512], Ob_sb[:, :],
                                 bout_row[0:1, n * 512:(n + 1) * 512],
                                 start=False, stop=True)
                return r
            M[f"pe_m3{c}"] = cop("tensor", f_m3,
                                  waits=[(AC, M[f"ac_yev{c}"]), (AC, M[f"ac_cb{c}"]),
                                         (WO, M["dw_wout"])], inc=PE)


        def emit_tail_b(c):
            p, xb, ob = _bufs(c)
            xn_tm, xn_fm = _xn(c)
            M[f"dv_g{c}"] = cop("vector",
                                 lambda e: e.tensor_tensor(sc1[:, 1024:2048], sc2[:, 0:1024],
                                                           sc3[:, :], ALU.mult),
                                 waits=[(AC, M[f"ac_sg{c}"])], inc=DV)

            wl = [(PE, M[f"pe_m3{c}"]), (DV, M[f"dv_g{c}"])]
            if c == 0:
                wl.append((PE, M["pe_tokdone"]))
            M[f"dv_fa{c}"] = cop("vector",
                                 lambda e: e.scalar_tensor_tensor(sc3[:, :], ps_zd[:, :], 1.0,
                                                                  sc1[:, 1024:2048],
                                                                  op0=ALU.mult, op1=ALU.mult),
                                 waits=wl, inc=DV)
            wl2 = [(DV, M[f"dv_fa{c}"])]
            if c >= 2:
                wl2.append((SO[p], 16 * (c // 2)))
            v_final[c] = cop("vector",
                             (lambda e, xb=xb, ob=ob: e.tensor_tensor(ob[:, :], sc3[:, :],
                                                                      xb[:, :], ALU.add)),
                             waits=wl2, inc=DV)
            M[f"dv_final{c}"] = v_final[c]

            if not _trunc[0]:
                cop("sync", (lambda e, ob=ob, c=c: e.dma_start(
                    out=out_d[c * 128:(c + 1) * 128, :], in_=ob[:, :])),
                     waits=[(DV, v_final[c])], inc=SO[p], n=16)
                nso[p] += 1
                if c + 2 < nch:
                    x_load(c + 2, [(DV, v_final[c])])

        if debug_taps:
            def f_dbg_memset(e):
                e.memset(stat[:, :], 0.0)
                e.memset(w_t[:, :], 0.0)
                return e.memset(VW[:, :], 0.0)
            S.ops["vector"].insert(0, ([], f_dbg_memset, None, 1))
            taps = dict(t_stat=stat, t_xn_tm=xn_tm, t_xn_fm=xn_fm, t_u=u_sb,
                        t_xig=xi_g, t_xid=xi_dp, t_expz=expz, t_sums=sums,
                        t_mxs=mxs, t_m6s=m6s, t_m6=m6, t_eq6=eq6, t_ec6=ec6,
                        t_gates=gates, t_rsum=rsum, t_gsc=gsc, t_w=w_sb,
                        t_wt=w_t, t_ccol=carry_col, t_ct=carry_t, t_sc1=sc1,
                        t_sc2=sc2, t_sc3=sc3, t_gb=gb, t_dpb=dpb, t_VW=VW,
                        t_TG=TG, t_svec=scale_vec, t_wdwd=wdw_dmaj)
            allw = [(PE, S.val(PE)), (AC, S.val(AC)), (DV, S.val(DV))]
            for tn, th in taps.items():
                shp = list(th.shape)
                dt_ = th.dtype
                od = nc.dram_tensor(tn, shp, dt_, kind="ExternalOutput").ap()
                S.op("sync", (lambda e, od=od, th=th: e.dma_start(out=od, in_=th[:, :])),
                     waits=allw, inc=SO[0], n=16)
                nso[0] += 1


        emit_front(0)
        emit_front_b(0)
        for c in range(nch):
            emit_body(c)
            if c + 1 < nch:
                emit_front(c + 1)
            emit_body_c(c)
            if c + 1 < nch:
                emit_front_b(c + 1)
            emit_tail_a(c)
            emit_tail_b(c)

        fw = []
        if nso[0]:
            fw.append((SO[0], 16 * nso[0]))
        if nso[1]:
            fw.append((SO[1], 16 * nso[1]))
        if not fw:
            fw = [(DW, 16), (DC, 16)]
        S.op("sync", lambda e: e.nop(), waits=fw)

        with nc.Block() as block:
            def make_section(name):
                def body(eng):
                    if name == "scalar":
                        from concourse.hw_specs import get_activation_tables
                        for idx, funcs in enumerate(get_activation_tables(nc.m.arch).values()):
                            if AF.Ln in funcs and AF.Exp in funcs:
                                eng.add_instruction(mybir.InstLoadActFuncSet(
                                    name=nc.get_next_instruction_name(),
                                    act_func_set_id=idx, ins=[], outs=[]))
                                break
                    for waits, fn, inc, n in S.ops[name]:
                        for smp, v in waits:
                            eng.wait_ge(smp, v)
                        inst = fn(eng)
                        if inc is not None:
                            inst.then_inc(inc, n)
                return body
            block.sync(make_section("sync"))
            block.gpsimd(make_section("gpsimd"))
            block.tensor(make_section("tensor"))
            block.scalar(make_section("scalar"))
            block.vector(make_section("vector"))

    return nc


_CONSTS = _host_consts()
_PROG = None


def kernel(L0_attn_x, L0_attn_Win, L0_attn_bin_, L0_attn_Bp, L0_attn_Cp,
           L0_attn_Dp, L0_attn_Wout, L0_attn_bout, L0_ffn_A, L0_ffn_B,
           L0_ffn_w_gate, L0_ffn_wt_w, L0_ffn_wt_b, L0_ffn_wd_w, L0_ffn_wd_b,
           L0_ffn_scale):
    global _PROG
    if _PROG is None:
        _PROG = build_program()
    nc = _PROG

    f = np.asarray
    shared = {
        "Win": f(L0_attn_Win, dtype=np.float32),
        "bin_": f(L0_attn_bin_, dtype=np.float32).reshape(1, DI),
        "Bp": f(L0_attn_Bp, dtype=np.float32),
        "Cp": f(L0_attn_Cp, dtype=np.float32),
        "Dp": f(L0_attn_Dp, dtype=np.float32).reshape(1, DI),
        "Wout": f(L0_attn_Wout, dtype=np.float32),
        "bout": f(L0_attn_bout, dtype=np.float32).reshape(1, D),
        "A": f(L0_ffn_A, dtype=np.float32),
        "B": f(L0_ffn_B, dtype=np.float32),
        "w_gate": f(L0_ffn_w_gate, dtype=np.float32),
        "wt_w": f(L0_ffn_wt_w, dtype=np.float32),
        "wt_b": f(L0_ffn_wt_b, dtype=np.float32).reshape(8, 128),
        "wd_w": f(L0_ffn_wd_w, dtype=np.float32),
        "wd_b": f(L0_ffn_wd_b, dtype=np.float32).reshape(1, D),
        "scale": f(L0_ffn_scale, dtype=np.float32).reshape(1, 1),
    }
    shared.update(_CONSTS)
    x = np.ascontiguousarray(f(L0_attn_x, dtype=np.float32))
    in_maps = []
    for i in range(NCORE):
        m = dict(shared)
        m["x"] = np.ascontiguousarray(x[i])
        in_maps.append(m)

    res = run_bass_kernel_spmd(nc, in_maps, core_ids=list(range(NCORE)))
    out = np.stack([np.asarray(r["out"], dtype=np.float32) for r in res.results], axis=0)
    return out


# version 33
# speedup vs baseline: 1.1936x; 1.1936x over previous
import numpy as np
import ml_dtypes
from contextlib import ExitStack

import concourse.bass as bass
import concourse.mybir as mybir
from concourse.bass_utils import run_bass_kernel_spmd

F32 = mybir.dt.float32
BF16 = mybir.dt.bfloat16
AF = mybir.ActivationFunctionType
ALU = mybir.AluOpType
AX = mybir.AxisListType
bfd = ml_dtypes.bfloat16

DECAY = 0.9
NCORE = 8
T_FULL = 2048
D = 1024
DI = 2048
NCH = 16
NSL = 16
E6 = 6
L100 = 100
EPS32 = float(np.finfo(np.float32).eps)

ENGINES = ("sync", "gpsimd", "tensor", "scalar", "vector")


def _host_consts():
    T_np = np.zeros((128, 128), dtype=np.float64)
    for t in range(128):
        T_np[: t + 1, t] = DECAY ** (t - np.arange(t + 1))
    d_vec = DECAY ** (np.arange(128, dtype=np.float64) + 1.0)
    E_np = np.zeros((16, 16 * 128), dtype=np.float64)
    for s in range(16):
        E_np[s, s * 128:(s + 1) * 128] = d_vec
    return {
        "cT": T_np.astype(np.float32),
        "cE": E_np.astype(np.float32),
        "cIb": np.eye(128, dtype=bfd),
        "cIf": np.eye(128, dtype=np.float32),
        "cOb": np.ones((1, 128), dtype=bfd),
        "cOf": np.ones((1, 128), dtype=np.float32),
        "cO16": np.ones((16, 1), dtype=np.float32),
    }


class Sched:

    def __init__(self):
        self.ops = {e: [] for e in ENGINES}
        self.cnt = {}

    def op(self, eng, fn, waits=(), inc=None, n=1):
        self.ops[eng].append((list(waits), fn, inc, n))
        if inc is not None:
            self.cnt[inc] = self.cnt.get(inc, 0) + n
            return self.cnt[inc]
        return None

    def val(self, sem):
        return self.cnt.get(sem, 0)


def build_program(debug_taps=False, nch=NCH, chop=10**9):
    nc = bass.Bass("TRN2", target_bir_lowering=False)

    def din(name, shape, dtype=F32):
        return nc.dram_tensor(name, shape, dtype, kind="ExternalInput").ap()

    x_d = din("x", [T_FULL, D])
    win_d = din("Win", [D, DI])
    bin_d = din("bin_", [1, DI])
    bp_d = din("Bp", [DI, 16])
    cp_d = din("Cp", [16, DI])
    dp_d = din("Dp", [1, DI])
    wout_d = din("Wout", [DI, D])
    bout_d = din("bout", [1, D])
    a_d = din("A", [E6, L100, 16])
    b_d = din("B", [E6, 16, D])
    wg_d = din("w_gate", [D, E6])
    wtw_d = din("wt_w", [D, D])
    wtb_d = din("wt_b", [8, 128])
    wdw_d = din("wd_w", [D, D])
    wdb_d = din("wd_b", [1, D])
    scale_d = din("scale", [1, 1])
    cT_d = din("cT", [128, 128], F32)
    cE_d = din("cE", [16, 2048], F32)
    cIb_d = din("cIb", [128, 128], BF16)
    cIf_d = din("cIf", [128, 128], F32)
    cOb_d = din("cOb", [1, 128], BF16)
    cOf_d = din("cOf", [1, 128], F32)
    cO16_d = din("cO16", [16, 1], F32)
    out_d = nc.dram_tensor("out", [T_FULL, D], F32, kind="ExternalOutput").ap()

    es = ExitStack()

    def sb(name, shape, dtype):
        return es.enter_context(nc.sbuf_tensor(name, shape, dtype))

    with es:
        sc1 = sb("sc1", [128, 2048], F32)
        sc2 = sb("sc2", [128, 2048], F32)
        u_sb = sb("u_sb", [128, 2048], BF16)
        xi_g = sb("xi_g", [128, 2048], BF16)
        xi_dp = sb("xi_dp", [128, 2048], BF16)
        scr4 = sb("scr4", [128, 1024], F32)
        x_buf0 = sb("x_buf0", [128, 1024], F32)
        x_buf1 = sb("x_buf1", [128, 1024], F32)
        out_buf0 = sb("out_buf0", [128, 1024], F32)
        out_buf1 = sb("out_buf1", [128, 1024], F32)
        xn_tm0 = sb("xn_tm0", [128, 1024], BF16)
        xn_tm1 = sb("xn_tm1", [128, 1024], BF16)
        xn_fm0 = sb("xn_fm0", [128, 1024], BF16)
        xn_fm1 = sb("xn_fm1", [128, 1024], BF16)
        expz = sb("expz", [128, 600], BF16)
        w_sb = sb("w_sb", [128, 600], BF16)
        w_t = sb("w_t", [128, 768], BF16)
        sc3 = sb("sc3", [128, 1024], F32)
        Win_sb = sb("Win_sb", [128, 8 * 2048], BF16)
        Wout_sb = sb("Wout_sb", [128, 16 * 1024], BF16)
        wdw_dmaj = sb("wdw_dmaj", [128, 8 * 1024], BF16)
        VW = sb("VW", [128, 6 * 1024], BF16)
        TG = sb("TG", [128, 8 * 606], BF16)
        gb = sb("gb", [128, 2048], BF16)
        dpb = sb("dpb", [128, 2048], BF16)
        T_sb = sb("T_sb", [128, 128], BF16)
        E_sb = sb("E_sb", [16, 2048], BF16)
        Ib_sb = sb("Ib_sb", [128, 128], BF16)
        If_sb = sb("If_sb", [128, 128], F32)
        Ob_sb = sb("Ob_sb", [1, 128], BF16)
        Of_sb = sb("Of_sb", [1, 128], F32)
        O16_sb = sb("O16_sb", [16, 1], F32)
        bin_row = sb("bin_row", [1, 2048], BF16)
        bout_row = sb("bout_row", [1, 1024], BF16)
        wdb_row = sb("wdb_row", [1, 1024], BF16)
        wtb_nat = sb("wtb_nat", [8, 128], BF16)
        wtb_col = sb("wtb_col", [128, 8], BF16)
        wg_nat = sb("wg_nat", [128, 48], BF16)
        bp_nat = sb("bp_nat", [128, 256], F32)
        a_nat = sb("a_nat", [100, 96], BF16)
        scale_sb = sb("scale_sb", [1, 1], F32)
        scale_vec = sb("scale_vec", [128, 1], F32)
        carry_t = sb("carry_t", [16, 128], BF16)
        carry_col = sb("carry_col", [128, 16], BF16)
        stat = sb("stat", [128, 8], F32)
        mxs = sb("mxs", [128, 6], F32)
        m6s = sb("m6s", [128, 6], F32)
        sums = sb("sums", [128, 6], F32)
        m6 = sb("m6", [128, 1], F32)
        eq6 = sb("eq6", [128, 6], F32)
        ec6 = sb("ec6", [128, 6], F32)
        ecz6 = sb("ecz6", [128, 6], F32)
        gates = sb("gates", [128, 6], F32)
        rsum = sb("rsum", [128, 6], F32)
        gsc = sb("gsc", [128, 6], F32)
        tmp6 = sb("tmp6", [128, 6], F32)

        def addr(h):
            return nc.lookup_mloc(h).addr

        def over(name, shape, dtype, host, off=0):
            return es.enter_context(
                _OverGuard(nc.alloc_sbuf_tensor_at(name, shape, dtype,
                                                   offset=addr(host) + off)))

        class _OverGuard:
            def __init__(self, h):
                self.h = h

            def __enter__(self):
                return self.h

            def __exit__(self, *a):
                return False

        assert addr(sc2) == addr(sc1) + 8192, (addr(sc1), addr(sc2))
        assert addr(xi_g) == addr(u_sb) + 4096
        assert addr(xi_dp) == addr(xi_g) + 4096
        assert addr(scr4) == addr(xi_dp) + 4096
        assert addr(x_buf1) == addr(x_buf0) + 4096
        assert addr(out_buf0) == addr(x_buf1) + 4096
        assert addr(xn_tm1) == addr(xn_tm0) + 2048

        Cp_sb = over("Cp_ov", [16, 2048], F32, sc1, 0)
        BpT_sb = over("BpT_ov", [16, 2048], F32, sc1, 8192)
        W2T = over("W2T_ov", [128, 8192], BF16, sc1, 0)
        wtw_nat = over("wtw_ov", [128, 8192], BF16, u_sb, 0)
        B_all = over("Ball_ov", [16, 6144], BF16, x_buf0, 0)
        prod_h = over("prod_ov", [16, 1024], F32, xn_tm0, 0)
        aT_sb = over("aT_ov", [16, 600], BF16, expz, 2048)
        B2_row = over("B2_ov", [1, 1024], BF16, expz, 0)
        Dp_row = over("Dprow_ov", [1, 2048], BF16, xi_dp, 0)
        g_row = over("grow_ov", [1, 2048], BF16, sc3, 0)
        wdw_nat = over("wdwnat_ov", [128, 8192], BF16, sc1, 0)
        wst = over("wst_ov", [128, 8192], F32, Wout_sb, 0)

        ps_big = es.enter_context(nc.psum_tensor("ps_big", [128, 2048], F32))
        ps_tp1 = es.enter_context(nc.psum_tensor("ps_tp1", [128, 1024], BF16))
        ps_tp2 = es.enter_context(nc.psum_tensor("ps_tp2", [128, 1024], BF16))
        ps_zd = es.enter_context(nc.psum_tensor("ps_zd", [128, 1024], F32))

        sem = {}
        for sname in ("dw", "dc", "dx0", "dx1", "so0", "so1", "pe", "ac", "dv",
                      "wdw", "wtw", "win", "wo"):
            sem[sname] = es.enter_context(nc.semaphore("s_" + sname))
        S = Sched()
        M = {}

        DW, DC = sem["dw"], sem["dc"]
        PE, AC, DV = sem["pe"], sem["ac"], sem["dv"]
        WDW, WTW, WIN, WO = sem["wdw"], sem["wtw"], sem["win"], sem["wo"]

        const_loads = [
            (Ib_sb, cIb_d), (If_sb, cIf_d),
            (Ob_sb, cOb_d), (Of_sb, cOf_d), (O16_sb, cO16_d),
            (bp_nat, bp_d.rearrange("(s p) j -> p s j", p=128)),
            (Cp_sb, cp_d), (scale_sb, scale_d),
        ]
        for dst, src in const_loads:
            S.op("sync", (lambda e, d=dst, s=src: e.dma_start(out=d[:, :] if d.shape[0] != 16 or True else d, in_=s)),
                 inc=DC, n=16)
        M["dc_all"] = 16 * len(const_loads)

        gq = []
        ndw = [0]

        def gdma(dst_ap, src_ap, waits=(), gsem=None):
            gsem = DW if gsem is None else gsem
            v = S.op("gpsimd", (lambda e, d=dst_ap, s=src_ap: e.dma_start(out=d, in_=s)),
                     waits=waits, inc=gsem, n=16)
            ndw[0] += 1
            return v

        M["dw_smalls"] = None
        gdma(T_sb[:, :], cT_d)
        gdma(E_sb[:, :], cE_d)
        gdma(bin_row[:, :], bin_d)
        gdma(bout_row[:, :], bout_d)
        gdma(wdb_row[:, :], wdb_d)
        gdma(Dp_row[:, :], dp_d)
        gdma(wtb_nat[:, :], wtb_d)
        gdma(wg_nat[:, :].rearrange("p (k e) -> p k e", k=8),
             wg_d.rearrange("(k p) e -> p k e", p=128))
        gdma(a_nat[:, :].rearrange("p (e r) -> p e r", e=6),
             a_d.rearrange("e l r -> l e r"))
        M["dw_smalls"] = gdma(B_all[:, :].rearrange("r (e c) -> r e c", e=6),
                              b_d.rearrange("e r c -> r e c"))
        M["dw_wdw"] = None

        def f_bptp(e):
            r = None
            for s in range(16):
                r = e.transpose(ps_big[0:16, s * 128:(s + 1) * 128],
                                bp_nat[:, s * 16:(s + 1) * 16], If_sb[:, :])
            return r
        M["pe_bptp"] = S.op("tensor", f_bptp, waits=[(DC, M["dc_all"])], inc=PE)

        M["ac_bpT"] = S.op("scalar",
                           lambda e: e.activation(BpT_sb[:, :], ps_big[0:16, :], AF.Identity, bias=0.0),
                           waits=[(PE, M["pe_bptp"])], inc=AC)

        def f_prod(e, h):
            return e.tensor_tensor(prod_h[:, :], BpT_sb[:, h * 1024:(h + 1) * 1024],
                                   Cp_sb[:, h * 1024:(h + 1) * 1024], ALU.mult)
        def f_grow(e, h):
            r = None
            for n in range(2):
                r = e.matmul(ps_big[0:1, (h * 2 + n) * 512:(h * 2 + n + 1) * 512],
                             O16_sb[:, :], prod_h[0:16, n * 512:(n + 1) * 512],
                             start=True, stop=True)
            return r
        M["dv_prod0"] = S.op("vector", lambda e: f_prod(e, 0),
                             waits=[(AC, M["ac_bpT"]), (DC, M["dc_all"])], inc=DV)
        M["pe_grow0"] = S.op("tensor", lambda e: f_grow(e, 0),
                             waits=[(DV, M["dv_prod0"])], inc=PE)
        M["dv_prod1"] = S.op("vector", lambda e: f_prod(e, 1),
                             waits=[(PE, M["pe_grow0"])], inc=DV)
        M["pe_grow1"] = S.op("tensor", lambda e: f_grow(e, 1),
                             waits=[(DV, M["dv_prod1"])], inc=PE)
        M["ac_grow"] = S.op("scalar",
                            lambda e: e.activation(g_row[:, :], ps_big[0:1, :], AF.Copy, bias=0.0),
                            waits=[(PE, M["pe_grow1"])], inc=AC)

        def f_bcast(e, row, n0):
            return e.matmul(ps_big[:, n0 * 512:(n0 + 1) * 512], Ob_sb[:, :],
                            row[0:1, n0 * 512:(n0 + 1) * 512], start=True, stop=True)
        def f_bc_all(e, row):
            r = None
            for n in range(4):
                r = f_bcast(e, row, n)
            return r
        M["pe_gbc"] = S.op("tensor", lambda e: f_bc_all(e, g_row),
                           waits=[(AC, M["ac_grow"])], inc=PE)
        M["ac_gb"] = S.op("scalar",
                          lambda e: e.activation(gb[:, :], ps_big[:, :], AF.Copy, bias=0.0),
                          waits=[(PE, M["pe_gbc"])], inc=AC)
        M["pe_dbc"] = S.op("tensor", lambda e: f_bc_all(e, Dp_row),
                           waits=[(AC, M["ac_gb"]), (DW, M["dw_smalls"])], inc=PE)
        M["ac_dpb"] = S.op("scalar",
                           lambda e: e.activation(dpb[:, :], ps_big[:, :], AF.Copy, bias=0.0),
                           waits=[(PE, M["pe_dbc"])], inc=AC)
        M["pe_sbc"] = S.op("tensor",
                           lambda e: e.matmul(ps_big[:, 0:1], Of_sb[:, :], scale_sb[:, :],
                                              start=True, stop=True),
                           waits=[(AC, M["ac_dpb"])], inc=PE)
        M["ac_svec"] = S.op("scalar",
                            lambda e: e.activation(scale_vec[:, :], ps_big[:, 0:1], AF.Identity, bias=0.0),
                            waits=[(PE, M["pe_sbc"])], inc=AC)

        M["pe_wtbtp"] = S.op("tensor",
                             lambda e: e.transpose(ps_tp1[:, 0:8], wtb_nat[0:8, :], Ib_sb[0:8, 0:8]),
                             waits=[(DW, M["dw_smalls"]), (DC, M["dc_all"])], inc=PE)
        M["dv_wtb"] = S.op("vector",
                           lambda e: e.tensor_copy(wtb_col[:, :], ps_tp1[:, 0:8]),
                           waits=[(PE, M["pe_wtbtp"])], inc=DV)

        def f_atp(e):
            r = None
            for ex in range(6):
                r = e.transpose(ps_tp2[0:16, ex * 100:(ex + 1) * 100],
                                a_nat[0:100, ex * 16:(ex + 1) * 16], Ib_sb[0:100, 0:100])
            return r
        M["pe_atp"] = S.op("tensor", f_atp, waits=[(DW, M["dw_smalls"])], inc=PE)
        M["ac_aT"] = S.op("scalar",
                          lambda e: e.activation(aT_sb[:, :], ps_tp2[0:16, 0:600], AF.Copy, bias=0.0),
                          waits=[(PE, M["pe_atp"])], inc=AC)

        for kc in range(8):
            def f_tokmm(e, kc=kc):
                r = None
                for ex in range(6):
                    slot = ex * 100 if ex < 5 else 512
                    r = e.matmul(ps_zd[:, slot:slot + 100],
                                 B_all[0:16, ex * 1024 + kc * 128: ex * 1024 + (kc + 1) * 128],
                                 aT_sb[0:16, ex * 100:(ex + 1) * 100],
                                 start=True, stop=True)
                return r
            wl = [(AC, M["ac_aT"]), (DW, M["dw_smalls"])]
            if kc > 0:
                wl.append((DV, M[f"dv_tok{kc-1}"]))
            M[f"pe_tok{kc}"] = S.op("tensor", f_tokmm, waits=wl, inc=PE)

            def f_tokev(e, kc=kc):
                e.tensor_copy(TG[:, kc * 606: kc * 606 + 500], ps_zd[:, 0:500])
                return e.tensor_copy(TG[:, kc * 606 + 500: kc * 606 + 600], ps_zd[:, 512:612])
            M[f"dv_tok{kc}"] = S.op("vector", f_tokev,
                                    waits=[(PE, M[f"pe_tok{kc}"])], inc=DV)
        M["pe_tokdone"] = M["pe_tok7"]
        M["dv_tokdone"] = M["dv_tok7"]

        M["dv_wg"] = S.op("vector",
                          lambda e: e.tensor_copy(
                              TG[:, :].rearrange("p (k c) -> p k c", k=8)[:, :, 600:606],
                              wg_nat[:, :].rearrange("p (k e) -> p k e", k=8)),
                          waits=[(DW, M["dw_smalls"]), (DV, M["dv_tokdone"])], inc=DV)

        S.op("sync", lambda e: e.dma_start(
                 out=wst[:, :].rearrange("p (k d) -> p k d", k=8),
                 in_=wdw_d.rearrange("(k p) d -> p k d", p=128)),
             inc=WDW, n=16)
        def f_wdwc(e):
            e.activation(wdw_nat[:, 0:4096], wst[:, 0:4096], AF.Copy, bias=0.0)
            return e.activation(wdw_nat[:, 4096:8192], wst[:, 4096:8192],
                                AF.Copy, bias=0.0)
        M["ac_wdwc"] = S.op("scalar", f_wdwc,
                            waits=[(WDW, 16), (DV, M["dv_prod1"])],
                            inc=AC)
        S.op("sync", lambda e: e.dma_start(
                 out=wst[:, :].rearrange("p (k d) -> p k d", k=8),
                 in_=wtw_d.rearrange("(k p) d -> p k d", p=128)),
             waits=[(AC, M["ac_wdwc"])], inc=WTW, n=16)
        def f_wtwc(e):
            e.activation(wtw_nat[:, 0:4096], wst[:, 0:4096], AF.Copy, bias=0.0)
            return e.activation(wtw_nat[:, 4096:8192], wst[:, 4096:8192],
                                AF.Copy, bias=0.0)
        M["ac_wtwc"] = S.op("scalar", f_wtwc,
                            waits=[(WTW, 16), (PE, M["pe_dbc"])],
                            inc=AC)
        for k in range(8):
            gdma(Win_sb[:, k * 2048:(k + 1) * 2048],
                 win_d[k * 128:(k + 1) * 128, :], gsem=WIN)
        M["dw_win"] = S.val(WIN)
        for k in range(16):
            gdma(Wout_sb[:, k * 1024:(k + 1) * 1024],
                 wout_d[k * 128:(k + 1) * 128, :],
                 waits=([(AC, M["ac_wtwc"])] if k == 0 else ()), gsem=WO)
        M["dw_wout"] = S.val(WO)

        for rj in range(8):
            def f_wdtp(e, rj=rj):
                r = None
                for k in range(8):
                    r = e.transpose(ps_tp1[:, k * 128:(k + 1) * 128],
                                    wdw_nat[:, k * 1024 + rj * 128: k * 1024 + (rj + 1) * 128],
                                    Ib_sb[:, :])
                return r
            wl = [(AC, M["ac_wdwc"]), (DV, M["dv_wtb"])]
            if rj > 0:
                wl.append((DV, M[f"dv_wdtp{rj-1}"]))
            M[f"pe_wdtp{rj}"] = S.op("tensor", f_wdtp, waits=wl, inc=PE)
            M[f"dv_wdtp{rj}"] = S.op("vector",
                                     lambda e, rj=rj: e.tensor_copy(
                                         wdw_dmaj[:, rj * 1024:(rj + 1) * 1024], ps_tp1[:, :]),
                                     waits=[(PE, M[f"pe_wdtp{rj}"])], inc=DV)
        M["pe_wdtp_done"] = M["pe_wdtp7"]
        M["dv_wdtp_done"] = M["dv_wdtp7"]

        w2t_seq = [(kcc, n) for kcc in range(8) for n in range(2)]
        for idx, (kcc, n) in enumerate(w2t_seq):
            bank = idx % 4

            def f_w2t(e, kcc=kcc, n=n, bank=bank):
                r = None
                for kd in range(8):
                    r = e.matmul(ps_big[:, bank * 512:(bank + 1) * 512],
                                 wtw_nat[:, kd * 1024 + kcc * 128: kd * 1024 + (kcc + 1) * 128],
                                 wdw_dmaj[:, kd * 1024 + n * 512: kd * 1024 + (n + 1) * 512],
                                 start=(kd == 0), stop=(kd == 7))
                return r
            wl = [(AC, M["ac_wtwc"]), (DV, M["dv_wdtp_done"]), (AC, M["ac_svec"])]
            if idx >= 4:
                wl.append((DV, M[f"dv_w2t{idx-4}"]))
            M[f"pe_w2t{idx}"] = S.op("tensor", f_w2t, waits=wl, inc=PE)
            M[f"dv_w2t{idx}"] = S.op("vector",
                                     lambda e, kcc=kcc, n=n, bank=bank: e.tensor_copy(
                                         W2T[:, kcc * 1024 + n * 512: kcc * 1024 + (n + 1) * 512],
                                         ps_big[:, bank * 512:(bank + 1) * 512]),
                                     waits=[(PE, M[f"pe_w2t{idx}"])], inc=DV)
        M["pe_w2t_done"] = M["pe_w2t15"]
        M["dv_w2t_done"] = M["dv_w2t15"]

        vw_seq = [(ex, n) for ex in range(6) for n in range(2)]
        for idx, (ex, n) in enumerate(vw_seq):
            bank = idx % 4

            def f_vw(e, ex=ex, n=n, bank=bank):
                r = None
                for kcc in range(8):
                    r = e.matmul(ps_big[0:100, bank * 512:(bank + 1) * 512],
                                 TG[:, kcc * 606 + ex * 100: kcc * 606 + (ex + 1) * 100],
                                 W2T[:, kcc * 1024 + n * 512: kcc * 1024 + (n + 1) * 512],
                                 start=(kcc == 0), stop=(kcc == 7))
                return r
            wl = [(DV, M["dv_w2t_done"]), (DV, M["dv_tokdone"])]
            if idx >= 4:
                wl.append((AC, M[f"ac_vw{idx-4}"]))
            M[f"pe_vw{idx}"] = S.op("tensor", f_vw, waits=wl, inc=PE)
            M[f"ac_vw{idx}"] = S.op("scalar",
                                    lambda e, ex=ex, n=n, bank=bank: e.activation(
                                        VW[0:100, ex * 1024 + n * 512: ex * 1024 + (n + 1) * 512],
                                        ps_big[0:100, bank * 512:(bank + 1) * 512],
                                        AF.Copy, bias=0.0),
                                    waits=[(PE, M[f"pe_vw{idx}"])], inc=AC)
        M["pe_vw_done"] = M["pe_vw11"]
        M["ac_vw_done"] = M["ac_vw11"]

        def f_b2(e):
            r = None
            for n in range(2):
                for kd in range(8):
                    r = e.matmul(ps_zd[0:1, n * 512:(n + 1) * 512],
                                 wtb_col[:, kd:kd + 1],
                                 wdw_dmaj[:, kd * 1024 + n * 512: kd * 1024 + (n + 1) * 512],
                                 start=(kd == 0), stop=(kd == 7))
            return r
        M["pe_b2"] = S.op("tensor", f_b2,
                          waits=[(AC, M["ac_vw_done"]), (DV, M["dv_wtb"])], inc=PE)
        M["ac_b2"] = S.op("scalar",
                          lambda e: e.activation(B2_row[:, :], ps_zd[0:1, :], AF.Copy, bias=0.0),
                          waits=[(PE, M["pe_b2"])], inc=AC)

        def f_b2vw(e):
            r = None
            for ex in range(6):
                r = e.activation(VW[0:1, ex * 1024:(ex + 1) * 1024], B2_row[:, :],
                                 AF.Copy, bias=0.0)
            return r
        M["ac_b2vw"] = S.op("scalar", f_b2vw, waits=[(AC, M["ac_b2"])], inc=AC)

        M["dv_ct0"] = S.op("vector", lambda e: e.memset(carry_t[:, :], 0.0), inc=DV)

        DX = [sem["dx0"], sem["dx1"]]
        SO = [sem["so0"], sem["so1"]]
        nx = [0, 0]
        nso = [0, 0]

        def x_load(c, waits):
            p = c % 2
            buf = x_buf0 if p == 0 else x_buf1
            S.op("sync", (lambda e, b=buf, c=c: e.dma_start(out=b[:, :],
                                                            in_=x_d[c * 128:(c + 1) * 128, :])),
                 waits=waits, inc=DX[p], n=16)
            nx[p] += 1
            return 16 * nx[p]

        M["x0"] = x_load(0, [(PE, M["pe_tokdone"])])
        M["x1"] = x_load(1, [(PE, M["pe_tokdone"])])

        v_final = {}
        _chopn = [0]
        _trunc = [False]

        def cop(*a, **kw):
            _chopn[0] += 1
            if _chopn[0] > chop:
                _trunc[0] = True
                return None
            return S.op(*a, **kw)
        def _bufs(c):
            p = c % 2
            return p, (x_buf0 if p == 0 else x_buf1), (out_buf0 if p == 0 else out_buf1)

        def _xn(c):
            p = c % 2
            return (xn_tm0 if p == 0 else xn_tm1), (xn_fm0 if p == 0 else xn_fm1)

        def emit_front(c):
            p, xb, ob = _bufs(c)

            M[f"ac_sq{c}"] = cop("scalar",
                                  lambda e, xb=xb: e.activation(scr4[:, :], xb[:, :], AF.Square),
                                  waits=[(DX[p], 16 * (c // 2 + 1))] +
                                        ([(PE, M["pe_w2t_done"])] if c == 0 else []) +
                                        ([(DV, v_final[c - 2])] if c >= 2 else []),
                                  inc=AC)

            def f_stats(e, xb=xb):
                e.reduce_sum(stat[:, 0:1], xb[:, :], axis=AX.X)
                return e.reduce_sum(stat[:, 1:2], scr4[:, :], axis=AX.X)
            M[f"dv_red{c}"] = cop("vector", f_stats,
                                   waits=[(AC, M[f"ac_sq{c}"]),
                                          (DX[p], 16 * (c // 2 + 1))], inc=DV)

            def f_m1(e):
                e.tensor_scalar(stat[:, 2:3], stat[:, 0:1], 1.0 / 1024.0, None, op0=ALU.mult)
                return e.tensor_scalar(stat[:, 3:4], stat[:, 1:2], 1.0 / 1024.0, None, op0=ALU.mult)
            M[f"dv_mv1{c}"] = cop("vector", f_m1, waits=[(DV, M[f"dv_red{c}"])], inc=DV)
            M[f"dv_mv2{c}"] = cop("vector",
                                  lambda e: e.tensor_tensor(stat[:, 4:5], stat[:, 2:3],
                                                            stat[:, 2:3], ALU.mult),
                                  waits=[(DV, M[f"dv_mv1{c}"])], inc=DV)
            M[f"dv_mv{c}"] = cop("vector",
                                 lambda e: e.tensor_tensor(stat[:, 3:4], stat[:, 3:4],
                                                           stat[:, 4:5], ALU.subtract),
                                 waits=[(DV, M[f"dv_mv2{c}"])], inc=DV)

            M[f"ac_lnv{c}"] = cop("scalar",
                                  lambda e: e.activation(stat[:, 5:6], stat[:, 3:4], AF.Ln,
                                                         bias=np.float32(1e-5)),
                                  waits=[(DV, M[f"dv_mv{c}"])], inc=AC)
            M[f"ac_rstd{c}"] = cop("scalar",
                                   lambda e: e.activation(stat[:, 6:7], stat[:, 5:6], AF.Exp,
                                                          scale=np.float32(-0.5)),
                                   waits=[(AC, M[f"ac_lnv{c}"])], inc=AC)

            xntm, xnfm = _xn(c)
            M[f"dv_xn{c}"] = cop("vector",
                                  (lambda e, xb=xb, xntm=xntm: e.tensor_scalar(
                                      xntm[:, :], xb[:, :], stat[:, 2:3], stat[:, 6:7],
                                      op0=ALU.subtract, op1=ALU.mult)),
                                  waits=[(AC, M[f"ac_rstd{c}"])] +
                                        ([(PE, M["pe_grow1"])] if c == 0 else []),
                                  inc=DV)

        def emit_body(c):
            p, xb, ob = _bufs(c)
            xn_tm, xn_fm = _xn(c)
            for h in range(2):
                def f_m1(e, h=h):
                    r = None
                    for kc in range(8):
                        lhs = xn_fm[:, kc * 128:(kc + 1) * 128]
                        for n in range(2 * h, 2 * h + 2):
                            e.matmul(ps_big[:, n * 512:(n + 1) * 512], lhs,
                                     Win_sb[:, kc * 2048 + n * 512: kc * 2048 + (n + 1) * 512],
                                     start=(kc == 0), stop=False)
                    for n in range(2 * h, 2 * h + 2):
                        r = e.matmul(ps_big[:, n * 512:(n + 1) * 512], Ob_sb[:, :],
                                     bin_row[0:1, n * 512:(n + 1) * 512],
                                     start=False, stop=True)
                    return r
                wl = [(DV, M[f"dv_xnfm{c}"]), (WIN, M["dw_win"])]
                if h == 0:
                    if c > 0:
                        wl.append((AC, M[f"ac_yev{c-1}"]))
                    else:
                        wl.append((AC, M["ac_vw_done"]))
                M[f"pe_m1{c}_{h}"] = cop("tensor", f_m1, waits=wl, inc=PE)
            M[f"pe_m1{c}"] = M[f"pe_m1{c}_1"]

            def f_z(e):
                r = None
                for kc in range(8):
                    lhs = xn_fm[:, kc * 128:(kc + 1) * 128]
                    e.matmul(ps_zd[:, 0:300], lhs,
                             TG[:, kc * 606: kc * 606 + 300],
                             start=(kc == 0), stop=(kc == 7))
                    r = e.matmul(ps_zd[:, 512:818], lhs,
                                 TG[:, kc * 606 + 300: kc * 606 + 606],
                                 start=(kc == 0), stop=(kc == 7))
                return r
            wl = [(DV, M[f"dv_xnfm{c}"])]
            if c > 0:
                wl.append((DV, v_final[c - 1]))
            else:
                wl.append((AC, M["ac_b2vw"]))
                wl.append((DV, M["dv_wg"]))
            M[f"pe_z{c}"] = cop("tensor", f_z, waits=wl, inc=PE)

            def emit_silu(h):
                H = slice(h * 1024, (h + 1) * 1024)
                wl = [(PE, M[f"pe_m1{c}_{h}"])]
                if c == 0:
                    wl.append((DV, M["dv_w2t_done"]))
                elif h == 0:
                    wl.append((DV, M[f"dv_fa{c-1}"]))
                M[f"ac_sigA{c}_{h}"] = cop("scalar",
                                      (lambda e, H=H: e.activation(sc1[:, H], ps_big[:, H],
                                                                   AF.Exp, scale=np.float32(-1.0))),
                                      waits=wl, inc=AC)
                M[f"ac_sigB{c}_{h}"] = cop("scalar",
                                      (lambda e, H=H: e.activation(sc2[:, H], sc1[:, H],
                                                                   AF.Ln, bias=np.float32(1.0))),
                                      waits=[(AC, M[f"ac_sigA{c}_{h}"])], inc=AC)
                M[f"ac_sig{c}_{h}"] = cop("scalar",
                                     (lambda e, H=H: e.activation(sc1[:, H], sc2[:, H],
                                                                  AF.Exp, scale=np.float32(-1.0))),
                                     waits=[(AC, M[f"ac_sigB{c}_{h}"])], inc=AC)
                wl = [(AC, M[f"ac_sig{c}_{h}"])]
                if c == 0 and h == 0:
                    wl.append((PE, M["pe_w2t_done"]))
                M[f"dv_u{c}_{h}"] = cop("vector",
                                    (lambda e, H=H: e.tensor_tensor(u_sb[:, H], ps_big[:, H],
                                                                    sc1[:, H], ALU.mult)),
                                    waits=wl, inc=DV)
                M[f"dv_presg{c}_{h}"] = cop("vector",
                                    (lambda e, H=H: e.tensor_tensor(xi_g[:, H], u_sb[:, H],
                                                                    gb[:, H], ALU.mult)),
                                    waits=[(DV, M[f"dv_u{c}_{h}"])] +
                                          ([(AC, M["ac_dpb"])] if c == 0 and h == 0 else []),
                                    inc=DV)

            emit_silu(0)

            M[f"dv_m6{c}"] = cop("vector",
                                 lambda e: e.reduce_max(m6[:, :], ps_zd[:, 812:818], axis=AX.X),
                                 waits=[(PE, M[f"pe_z{c}"])], inc=DV)
            prev = M[f"dv_m6{c}"]
            def st1(e):
                e.tensor_scalar(eq6[:, :], ps_zd[:, 812:818], m6[:, 0:1], None, op0=ALU.is_equal)
                return e.memset(ec6[:, 0:1], 0.0)
            prev = cop("vector", st1, waits=[(DV, prev)], inc=DV)
            for ex in range(1, 6):
                prev = cop("vector",
                           (lambda e, ex=ex: e.tensor_tensor(ec6[:, ex:ex + 1], ec6[:, ex - 1:ex],
                                                             eq6[:, ex - 1:ex], ALU.add)),
                           waits=[(DV, prev)], inc=DV)
            prev = cop("vector",
                       lambda e: e.tensor_scalar(ecz6[:, :], ec6[:, :], 0.0, None, op0=ALU.is_equal),
                       waits=[(DV, prev)], inc=DV)
            M[f"dv_gate6{c}"] = cop("vector",
                       lambda e: e.tensor_tensor(gates[:, :], eq6[:, :], ecz6[:, :], ALU.mult),
                       waits=[(DV, prev)], inc=DV)

            emit_silu(1)
            M[f"ac_sig{c}"] = M[f"ac_sig{c}_1"]

            def f_exps(e):
                r = None
                for ex in range(6):
                    off = ex * 100 if ex < 3 else 512 + (ex - 3) * 100
                    r = e.activation(expz[:, ex * 100:(ex + 1) * 100],
                                     ps_zd[:, off:off + 100], AF.Exp,
                                     scale=np.float32(1.0 / 32.0),
                                     accum_out=sums[:, ex:ex + 1])
                return r
            wl = [(PE, M[f"pe_z{c}"])]
            if c == 0:
                wl.append((AC, M["ac_aT"]))
                wl.append((AC, M["ac_b2vw"]))
            M[f"ac_exps{c}"] = cop("scalar", f_exps, waits=wl, inc=AC)


            prev = cop("vector",
                       lambda e: e.reciprocal(rsum[:, :], sums[:, :]),
                       waits=[(DV, M[f"dv_gate6{c}"]), (AC, M[f"ac_exps{c}"])], inc=DV)
            M[f"dv_gsc{c}"] = cop("vector",
                                  lambda e: e.tensor_tensor(gsc[:, :], gates[:, :], rsum[:, :], ALU.mult),
                                  waits=[(DV, prev)], inc=DV)

            def f_wbuild(e):
                r = None
                for ex in range(6):
                    r = e.tensor_scalar(w_sb[:, ex * 100:(ex + 1) * 100],
                                        expz[:, ex * 100:(ex + 1) * 100],
                                        gsc[:, ex:ex + 1], None, op0=ALU.mult)
                return r
            M[f"dv_w0{c}"] = cop("vector", f_wbuild, waits=[(DV, M[f"dv_gsc{c}"])], inc=DV)
            M[f"dv_w1{c}"] = cop("vector",
                                 lambda e: e.tensor_tensor(tmp6[:, :], gates[:, :],
                                                           w_sb[:, 0:600:100], ALU.subtract),
                                 waits=[(DV, M[f"dv_w0{c}"])], inc=DV)
            M[f"dv_w{c}"] = cop("vector",
                                lambda e: e.tensor_copy(w_sb[:, 0:600:100], tmp6[:, :]),
                                waits=[(DV, M[f"dv_w1{c}"])], inc=DV)

            for h in range(2):
                H = slice(h * 1024, (h + 1) * 1024)
                wl = [(DV, M[f"dv_u{c}_{h}"])]
                if c == 0 and h == 0:
                    wl.append((AC, M["ac_dpb"]))
                M[f"dv_pres{c}_{h}"] = cop("vector",
                                    (lambda e, H=H: e.tensor_tensor(xi_dp[:, H], u_sb[:, H],
                                                                    dpb[:, H], ALU.mult)),
                                    waits=wl, inc=DV)
            M[f"dv_pres{c}"] = M[f"dv_pres{c}_1"]

            for h in range(2):
                def f_corr(e, c=c, h=h):
                    r = None
                    for b in range(2 * h, 2 * h + 2):
                        r = e.matmul(ps_big[:, b * 512:(b + 1) * 512], carry_t[:, :],
                                     E_sb[:, b * 512:(b + 1) * 512], start=True, stop=False)
                    return r
                wl = [(DV, M[f"dv_u{c}_{h}"])]
                if h == 0:
                    if c > 0:
                        wl.append((AC, M[f"ac_ct{c-1}"]))
                    else:
                        wl.append((DV, M["dv_ct0"]))
                M[f"pe_corr{c}_{h}"] = cop("tensor", f_corr, waits=wl, inc=PE)
                def f_tmm(e, c=c, h=h):
                    r = None
                    for s in range(8 * h, 8 * h + 8):
                        r = e.matmul(ps_big[:, s * 128:(s + 1) * 128],
                                     xi_g[:, s * 128:(s + 1) * 128], T_sb[:, :],
                                     start=False, stop=(s % 4 == 3))
                    return r
                M[f"pe_scan{c}_{h}"] = cop("tensor", f_tmm,
                                           waits=[(DV, M[f"dv_presg{c}_{h}"])], inc=PE)
            M[f"pe_scan{c}"] = M[f"pe_scan{c}_1"]

            M[f"ac_ext{c}"] = cop("scalar",
                                   lambda e: e.activation(carry_col[:, :], ps_big[:, 127:2048:128],
                                                          AF.Copy, bias=0.0),
                                   waits=[(PE, M[f"pe_scan{c}"])], inc=AC)

            M[f"pe_ctp{c}"] = cop("tensor",
                                   lambda e: e.transpose(ps_tp1[0:16, 896:1024],
                                                         carry_col[:, :], Ib_sb[:, :]),
                                   waits=[(AC, M[f"ac_ext{c}"])], inc=PE)
            M[f"ac_ct{c}"] = cop("scalar",
                                  lambda e: e.activation(carry_t[:, :], ps_tp1[0:16, 896:1024],
                                                         AF.Copy, bias=0.0),
                                  waits=[(PE, M[f"pe_ctp{c}"])], inc=AC)

            def f_wtp(e):
                r = None
                for ex in range(6):
                    r = e.transpose(ps_tp2[0:100, ex * 128:(ex + 1) * 128],
                                    w_sb[:, ex * 100:(ex + 1) * 100], Ib_sb[:, :])
                return r
            wl = [(DV, M[f"dv_w{c}"])]
            if c > 0:
                wl.append((AC, M[f"ac_wt{c-1}"]))
            else:
                wl.append((AC, M["ac_aT"]))
            M[f"pe_wtp{c}"] = cop("tensor", f_wtp, waits=wl, inc=PE)

            M[f"ac_wt{c}"] = cop("scalar",
                                  lambda e: e.activation(w_t[0:100, :], ps_tp2[0:100, 0:768],
                                                         AF.Copy, bias=0.0),
                                  waits=[(PE, M[f"pe_wtp{c}"])], inc=AC)

            def f_imm(e):
                r = None
                for s in range(NSL):
                    r = e.matmul(ps_big[:, s * 128:(s + 1) * 128],
                                 xi_dp[:, s * 128:(s + 1) * 128], Ib_sb[:, :],
                                 start=False, stop=True, skip_group_check=True)
                return r
            M[f"pe_imm{c}"] = cop("tensor", f_imm,
                                   waits=[(AC, M[f"ac_ext{c}"]), (DV, M[f"dv_pres{c}"])], inc=PE)

            M[f"ac_yev{c}"] = cop("scalar",
                                   lambda e: e.activation(u_sb[:, :], ps_big[:, :],
                                                          AF.Copy, bias=0.0),
                                   waits=[(PE, M[f"pe_imm{c}"]), (DV, M[f"dv_pres{c}"])], inc=AC)


        def emit_body_c(c):
            p, xb, ob = _bufs(c)
            xn_tm, xn_fm = _xn(c)
            def f_delta(e):
                r = None
                for kc in range(8):
                    lhs = xn_fm[:, kc * 128:(kc + 1) * 128]
                    for n in range(2):
                        r = e.matmul(ps_zd[:, n * 512:(n + 1) * 512], lhs,
                                     wdw_dmaj[:, kc * 1024 + n * 512: kc * 1024 + (n + 1) * 512],
                                     start=(kc == 0), stop=False)
                for ex in range(6):
                    lhs = w_t[0:100, ex * 128:(ex + 1) * 128]
                    for n in range(2):
                        r = e.matmul(ps_zd[:, n * 512:(n + 1) * 512], lhs,
                                     VW[0:100, ex * 1024 + n * 512: ex * 1024 + (n + 1) * 512],
                                     start=False, stop=False)
                for n in range(2):
                    r = e.matmul(ps_zd[:, n * 512:(n + 1) * 512], Ob_sb[:, :],
                                 wdb_row[0:1, n * 512:(n + 1) * 512],
                                 start=False, stop=True)
                return r
            wl = [(AC, M[f"ac_wt{c}"]), (AC, M[f"ac_exps{c}"]), (DV, M[f"dv_gsc{c}"])]
            if c == 0:
                wl.append((AC, M["ac_b2vw"]))
            M[f"pe_delta{c}"] = cop("tensor", f_delta, waits=wl, inc=PE)

            M[f"ac_cb{c}"] = cop("scalar",
                                  lambda e: e.activation(sc3[:, :], ps_zd[:, :], AF.Exp,
                                                         scale=scale_vec[:, 0:1]),
                                  waits=[(PE, M[f"pe_delta{c}"]), (AC, M["ac_svec"])], inc=AC)

            M[f"dv_epsA{c}"] = cop("vector",
                                   lambda e: e.tensor_scalar(sc2[:, 1024:2048], sc3[:, :], 0.0,
                                                             EPS32, op0=ALU.is_equal, op1=ALU.mult),
                                   waits=[(AC, M[f"ac_cb{c}"])], inc=DV)
            M[f"dv_eps{c}"] = cop("vector",
                                  lambda e: e.tensor_tensor(sc2[:, 0:1024], sc3[:, :],
                                                            sc2[:, 1024:2048], ALU.add),
                                  waits=[(DV, M[f"dv_epsA{c}"])], inc=DV)

            M[f"ac_ln{c}"] = cop("scalar",
                                  lambda e: e.activation(sc1[:, 0:1024], sc2[:, 0:1024], AF.Ln),
                                  waits=[(DV, M[f"dv_eps{c}"])], inc=AC)

            M[f"dv_ps{c}"] = cop("vector",
                                  lambda e: e.tensor_tensor(sc2[:, 0:1024], sc1[:, 0:1024],
                                                            xn_tm[:, :], ALU.add),
                                  waits=[(AC, M[f"ac_ln{c}"])], inc=DV)


        def emit_front_b(c):
            p, xb, ob = _bufs(c)
            xn_tm, xn_fm = _xn(c)
            def f_xntp(e):
                r = None
                for k in range(8):
                    r = e.transpose(ps_tp1[:, k * 128:(k + 1) * 128],
                                    xn_tm[:, k * 128:(k + 1) * 128], Ib_sb[:, :])
                return r
            wl = [(DV, M[f"dv_xn{c}"])]
            if c > 0:
                wl.append((DV, M[f"dv_xnfm{c-1}"]))
                wl.append((AC, M[f"ac_ct{c-1}"]))
            else:
                wl.append((DV, M["dv_wdtp_done"]))
            M[f"pe_xntp{c}"] = cop("tensor", f_xntp, waits=wl, inc=PE)

            M[f"dv_xnfm{c}"] = cop("vector",
                                    lambda e: e.tensor_copy(xn_fm[:, :], ps_tp1[:, :]),
                                    waits=[(PE, M[f"pe_xntp{c}"])], inc=DV)



        def emit_tail_a(c):
            p, xb, ob = _bufs(c)
            xn_tm, xn_fm = _xn(c)
            M[f"ac_sgA{c}"] = cop("scalar",
                                  lambda e: e.activation(sc3[:, :], sc2[:, 0:1024], AF.Exp,
                                                         scale=np.float32(-1.0)),
                                  waits=[(DV, M[f"dv_ps{c}"])], inc=AC)
            M[f"ac_sgB{c}"] = cop("scalar",
                                  lambda e: e.activation(sc1[:, 1024:2048], sc3[:, :], AF.Ln,
                                                         bias=np.float32(1.0)),
                                  waits=[(AC, M[f"ac_sgA{c}"])], inc=AC)
            M[f"ac_sg{c}"] = cop("scalar",
                                 lambda e: e.activation(sc3[:, :], sc1[:, 1024:2048], AF.Exp,
                                                        scale=np.float32(-1.0)),
                                 waits=[(AC, M[f"ac_sgB{c}"])], inc=AC)

            def f_m3(e):
                r = None
                for k in range(16):
                    lhs = u_sb[:, k * 128:(k + 1) * 128]
                    for n in range(2):
                        r = e.matmul(ps_zd[:, n * 512:(n + 1) * 512], lhs,
                                     Wout_sb[:, k * 1024 + n * 512: k * 1024 + (n + 1) * 512],
                                     start=(k == 0), stop=False)
                for n in range(2):
                    r = e.matmul(ps_zd[:, n * 512:(n + 1) * 512], Ob_sb[:, :],
                                 bout_row[0:1, n * 512:(n + 1) * 512],
                                 start=False, stop=True)
                return r
            M[f"pe_m3{c}"] = cop("tensor", f_m3,
                                  waits=[(AC, M[f"ac_yev{c}"]), (AC, M[f"ac_cb{c}"]),
                                         (WO, M["dw_wout"])], inc=PE)


        def emit_tail_b(c):
            p, xb, ob = _bufs(c)
            xn_tm, xn_fm = _xn(c)
            M[f"dv_g{c}"] = cop("vector",
                                 lambda e: e.tensor_tensor(sc1[:, 1024:2048], sc2[:, 0:1024],
                                                           sc3[:, :], ALU.mult),
                                 waits=[(AC, M[f"ac_sg{c}"])], inc=DV)

            wl = [(PE, M[f"pe_m3{c}"]), (DV, M[f"dv_g{c}"])]
            if c == 0:
                wl.append((PE, M["pe_tokdone"]))
            M[f"dv_fa{c}"] = cop("vector",
                                 lambda e: e.scalar_tensor_tensor(sc3[:, :], ps_zd[:, :], 1.0,
                                                                  sc1[:, 1024:2048],
                                                                  op0=ALU.mult, op1=ALU.mult),
                                 waits=wl, inc=DV)
            wl2 = [(DV, M[f"dv_fa{c}"])]
            if c >= 2:
                wl2.append((SO[p], 16 * (c // 2)))
            v_final[c] = cop("vector",
                             (lambda e, xb=xb, ob=ob: e.tensor_tensor(ob[:, :], sc3[:, :],
                                                                      xb[:, :], ALU.add)),
                             waits=wl2, inc=DV)
            M[f"dv_final{c}"] = v_final[c]

            if not _trunc[0]:
                cop("sync", (lambda e, ob=ob, c=c: e.dma_start(
                    out=out_d[c * 128:(c + 1) * 128, :], in_=ob[:, :])),
                     waits=[(DV, v_final[c])], inc=SO[p], n=16)
                nso[p] += 1
                if c + 2 < nch:
                    x_load(c + 2, [(DV, v_final[c])])

        if debug_taps:
            def f_dbg_memset(e):
                e.memset(stat[:, :], 0.0)
                e.memset(w_t[:, :], 0.0)
                return e.memset(VW[:, :], 0.0)
            S.ops["vector"].insert(0, ([], f_dbg_memset, None, 1))
            taps = dict(t_stat=stat, t_xn_tm=xn_tm, t_xn_fm=xn_fm, t_u=u_sb,
                        t_xig=xi_g, t_xid=xi_dp, t_expz=expz, t_sums=sums,
                        t_mxs=mxs, t_m6s=m6s, t_m6=m6, t_eq6=eq6, t_ec6=ec6,
                        t_gates=gates, t_rsum=rsum, t_gsc=gsc, t_w=w_sb,
                        t_wt=w_t, t_ccol=carry_col, t_ct=carry_t, t_sc1=sc1,
                        t_sc2=sc2, t_sc3=sc3, t_gb=gb, t_dpb=dpb, t_VW=VW,
                        t_TG=TG, t_svec=scale_vec, t_wdwd=wdw_dmaj)
            allw = [(PE, S.val(PE)), (AC, S.val(AC)), (DV, S.val(DV))]
            for tn, th in taps.items():
                shp = list(th.shape)
                dt_ = th.dtype
                od = nc.dram_tensor(tn, shp, dt_, kind="ExternalOutput").ap()
                S.op("sync", (lambda e, od=od, th=th: e.dma_start(out=od, in_=th[:, :])),
                     waits=allw, inc=SO[0], n=16)
                nso[0] += 1


        emit_front(0)
        emit_front_b(0)
        for c in range(nch):
            emit_body(c)
            if c + 1 < nch:
                emit_front(c + 1)
            emit_body_c(c)
            if c + 1 < nch:
                emit_front_b(c + 1)
            emit_tail_a(c)
            emit_tail_b(c)

        fw = []
        if nso[0]:
            fw.append((SO[0], 16 * nso[0]))
        if nso[1]:
            fw.append((SO[1], 16 * nso[1]))
        if not fw:
            fw = [(DW, 16), (DC, 16)]
        S.op("sync", lambda e: e.nop(), waits=fw)

        with nc.Block() as block:
            def make_section(name):
                def body(eng):
                    if name == "scalar":
                        from concourse.hw_specs import get_activation_tables
                        for idx, funcs in enumerate(get_activation_tables(nc.m.arch).values()):
                            if AF.Ln in funcs and AF.Exp in funcs:
                                eng.add_instruction(mybir.InstLoadActFuncSet(
                                    name=nc.get_next_instruction_name(),
                                    act_func_set_id=idx, ins=[], outs=[]))
                                break
                    for waits, fn, inc, n in S.ops[name]:
                        for smp, v in waits:
                            eng.wait_ge(smp, v)
                        inst = fn(eng)
                        if inc is not None:
                            inst.then_inc(inc, n)
                return body
            block.sync(make_section("sync"))
            block.gpsimd(make_section("gpsimd"))
            block.tensor(make_section("tensor"))
            block.scalar(make_section("scalar"))
            block.vector(make_section("vector"))

    return nc


_CONSTS = _host_consts()
_PROG = None


def kernel(L0_attn_x, L0_attn_Win, L0_attn_bin_, L0_attn_Bp, L0_attn_Cp,
           L0_attn_Dp, L0_attn_Wout, L0_attn_bout, L0_ffn_A, L0_ffn_B,
           L0_ffn_w_gate, L0_ffn_wt_w, L0_ffn_wt_b, L0_ffn_wd_w, L0_ffn_wd_b,
           L0_ffn_scale):
    global _PROG
    if _PROG is None:
        _PROG = build_program()
    nc = _PROG

    f = np.asarray
    shared = {
        "Win": f(L0_attn_Win, dtype=np.float32),
        "bin_": f(L0_attn_bin_, dtype=np.float32).reshape(1, DI),
        "Bp": f(L0_attn_Bp, dtype=np.float32),
        "Cp": f(L0_attn_Cp, dtype=np.float32),
        "Dp": f(L0_attn_Dp, dtype=np.float32).reshape(1, DI),
        "Wout": f(L0_attn_Wout, dtype=np.float32),
        "bout": f(L0_attn_bout, dtype=np.float32).reshape(1, D),
        "A": f(L0_ffn_A, dtype=np.float32),
        "B": f(L0_ffn_B, dtype=np.float32),
        "w_gate": f(L0_ffn_w_gate, dtype=np.float32),
        "wt_w": f(L0_ffn_wt_w, dtype=np.float32),
        "wt_b": f(L0_ffn_wt_b, dtype=np.float32).reshape(8, 128),
        "wd_w": f(L0_ffn_wd_w, dtype=np.float32),
        "wd_b": f(L0_ffn_wd_b, dtype=np.float32).reshape(1, D),
        "scale": f(L0_ffn_scale, dtype=np.float32).reshape(1, 1),
    }
    shared.update(_CONSTS)
    x = np.ascontiguousarray(f(L0_attn_x, dtype=np.float32))
    in_maps = []
    for i in range(NCORE):
        m = dict(shared)
        m["x"] = np.ascontiguousarray(x[i])
        in_maps.append(m)

    res = run_bass_kernel_spmd(nc, in_maps, core_ids=list(range(NCORE)))
    out = np.stack([np.asarray(r["out"], dtype=np.float32) for r in res.results], axis=0)
    return out


# version 35
# speedup vs baseline: 1.2188x; 1.0211x over previous
import numpy as np
import ml_dtypes
from contextlib import ExitStack

import concourse.bass as bass
import concourse.mybir as mybir
from concourse.bass_utils import run_bass_kernel_spmd

F32 = mybir.dt.float32
BF16 = mybir.dt.bfloat16
AF = mybir.ActivationFunctionType
ALU = mybir.AluOpType
AX = mybir.AxisListType
bfd = ml_dtypes.bfloat16

DECAY = 0.9
NCORE = 8
T_FULL = 2048
D = 1024
DI = 2048
NCH = 16
NSL = 16
E6 = 6
L100 = 100
EPS32 = float(np.finfo(np.float32).eps)

ENGINES = ("sync", "gpsimd", "tensor", "scalar", "vector")


def _host_consts():
    T_np = np.zeros((128, 128), dtype=np.float64)
    for t in range(128):
        T_np[: t + 1, t] = DECAY ** (t - np.arange(t + 1))
    d_vec = DECAY ** (np.arange(128, dtype=np.float64) + 1.0)
    E_np = np.zeros((16, 16 * 128), dtype=np.float64)
    for s in range(16):
        E_np[s, s * 128:(s + 1) * 128] = d_vec
    return {
        "cT": T_np.astype(np.float32),
        "cE": E_np.astype(np.float32),
        "cIb": np.eye(128, dtype=bfd),
        "cIf": np.eye(128, dtype=np.float32),
        "cOb": np.ones((1, 128), dtype=bfd),
        "cOf": np.ones((1, 128), dtype=np.float32),
        "cO16": np.ones((16, 1), dtype=np.float32),
    }


class Sched:

    def __init__(self):
        self.ops = {e: [] for e in ENGINES}
        self.cnt = {}

    def op(self, eng, fn, waits=(), inc=None, n=1):
        self.ops[eng].append((list(waits), fn, inc, n))
        if inc is not None:
            self.cnt[inc] = self.cnt.get(inc, 0) + n
            return self.cnt[inc]
        return None

    def val(self, sem):
        return self.cnt.get(sem, 0)


def build_program(debug_taps=False, nch=NCH, chop=10**9):
    nc = bass.Bass("TRN2", target_bir_lowering=False)

    def din(name, shape, dtype=F32):
        return nc.dram_tensor(name, shape, dtype, kind="ExternalInput").ap()

    x_d = din("x", [T_FULL, D])
    win_d = din("Win", [D, DI])
    bin_d = din("bin_", [1, DI])
    bp_d = din("Bp", [DI, 16])
    cp_d = din("Cp", [16, DI])
    dp_d = din("Dp", [1, DI])
    wout_d = din("Wout", [DI, D])
    bout_d = din("bout", [1, D])
    a_d = din("A", [E6, L100, 16])
    b_d = din("B", [E6, 16, D])
    wg_d = din("w_gate", [D, E6])
    wtw_d = din("wt_w", [D, D])
    wtb_d = din("wt_b", [8, 128])
    wdw_d = din("wd_w", [D, D])
    wdb_d = din("wd_b", [1, D])
    scale_d = din("scale", [1, 1])
    cT_d = din("cT", [128, 128], F32)
    cE_d = din("cE", [16, 2048], F32)
    cIb_d = din("cIb", [128, 128], BF16)
    cIf_d = din("cIf", [128, 128], F32)
    cOb_d = din("cOb", [1, 128], BF16)
    cOf_d = din("cOf", [1, 128], F32)
    cO16_d = din("cO16", [16, 1], F32)
    out_d = nc.dram_tensor("out", [T_FULL, D], F32, kind="ExternalOutput").ap()

    es = ExitStack()

    def sb(name, shape, dtype):
        return es.enter_context(nc.sbuf_tensor(name, shape, dtype))

    with es:
        sc1 = sb("sc1", [128, 2048], F32)
        sc2 = sb("sc2", [128, 2048], F32)
        u_sb = sb("u_sb", [128, 2048], BF16)
        xi_g = sb("xi_g", [128, 2048], BF16)
        xi_dp = sb("xi_dp", [128, 2048], BF16)
        scr4 = sb("scr4", [128, 1024], F32)
        x_buf0 = sb("x_buf0", [128, 1024], F32)
        x_buf1 = sb("x_buf1", [128, 1024], F32)
        out_buf0 = sb("out_buf0", [128, 1024], F32)
        out_buf1 = sb("out_buf1", [128, 1024], F32)
        xn_tm0 = sb("xn_tm0", [128, 1024], BF16)
        xn_tm1 = sb("xn_tm1", [128, 1024], BF16)
        xn_fm0 = sb("xn_fm0", [128, 1024], BF16)
        xn_fm1 = sb("xn_fm1", [128, 1024], BF16)
        expz = sb("expz", [128, 600], BF16)
        w_sb = sb("w_sb", [128, 600], BF16)
        w_t = sb("w_t", [128, 768], BF16)
        sc3 = sb("sc3", [128, 1024], F32)
        Win_sb = sb("Win_sb", [128, 8 * 2048], BF16)
        Wout_sb = sb("Wout_sb", [128, 16 * 1024], BF16)
        wdw_dmaj = sb("wdw_dmaj", [128, 8 * 1024], BF16)
        VW = sb("VW", [128, 6 * 1024], BF16)
        TG = sb("TG", [128, 8 * 606], BF16)
        gb = sb("gb", [128, 2048], BF16)
        dpb = sb("dpb", [128, 2048], BF16)
        T_sb = sb("T_sb", [128, 128], BF16)
        E_sb = sb("E_sb", [16, 2048], BF16)
        Ib_sb = sb("Ib_sb", [128, 128], BF16)
        If_sb = sb("If_sb", [128, 128], F32)
        Ob_sb = sb("Ob_sb", [1, 128], BF16)
        Of_sb = sb("Of_sb", [1, 128], F32)
        O16_sb = sb("O16_sb", [16, 1], F32)
        bin_row = sb("bin_row", [1, 2048], BF16)
        bout_row = sb("bout_row", [1, 1024], BF16)
        wdb_row = sb("wdb_row", [1, 1024], BF16)
        wtb_nat = sb("wtb_nat", [8, 128], BF16)
        wtb_col = sb("wtb_col", [128, 8], BF16)
        wg_nat = sb("wg_nat", [128, 48], BF16)
        bp_nat = sb("bp_nat", [128, 256], F32)
        a_nat = sb("a_nat", [100, 96], BF16)
        scale_sb = sb("scale_sb", [1, 1], F32)
        scale_vec = sb("scale_vec", [128, 1], F32)
        carry_t = sb("carry_t", [16, 128], BF16)
        carry_col = sb("carry_col", [128, 16], BF16)
        stat = sb("stat", [128, 8], F32)
        mxs = sb("mxs", [128, 6], F32)
        m6s = sb("m6s", [128, 6], F32)
        sums = sb("sums", [128, 6], F32)
        m6 = sb("m6", [128, 1], F32)
        eq6 = sb("eq6", [128, 6], F32)
        ec6 = sb("ec6", [128, 6], F32)
        ecz6 = sb("ecz6", [128, 6], F32)
        gates = sb("gates", [128, 6], F32)
        rsum = sb("rsum", [128, 6], F32)
        gsc = sb("gsc", [128, 6], F32)
        tmp6 = sb("tmp6", [128, 6], F32)

        def addr(h):
            return nc.lookup_mloc(h).addr

        def over(name, shape, dtype, host, off=0):
            return es.enter_context(
                _OverGuard(nc.alloc_sbuf_tensor_at(name, shape, dtype,
                                                   offset=addr(host) + off)))

        class _OverGuard:
            def __init__(self, h):
                self.h = h

            def __enter__(self):
                return self.h

            def __exit__(self, *a):
                return False

        assert addr(sc2) == addr(sc1) + 8192, (addr(sc1), addr(sc2))
        assert addr(xi_g) == addr(u_sb) + 4096
        assert addr(xi_dp) == addr(xi_g) + 4096
        assert addr(scr4) == addr(xi_dp) + 4096
        assert addr(x_buf1) == addr(x_buf0) + 4096
        assert addr(out_buf0) == addr(x_buf1) + 4096
        assert addr(xn_tm1) == addr(xn_tm0) + 2048

        Cp_sb = over("Cp_ov", [16, 2048], F32, sc1, 0)
        BpT_sb = over("BpT_ov", [16, 2048], F32, sc1, 8192)
        W2T = over("W2T_ov", [128, 8192], BF16, sc1, 0)
        wtw_nat = over("wtw_ov", [128, 8192], BF16, u_sb, 0)
        B_all = over("Ball_ov", [16, 6144], BF16, x_buf0, 0)
        prod_h = over("prod_ov", [16, 1024], F32, xn_tm0, 0)
        aT_sb = over("aT_ov", [16, 600], BF16, expz, 2048)
        B2_row = over("B2_ov", [1, 1024], BF16, expz, 0)
        Dp_row = over("Dprow_ov", [1, 2048], BF16, xi_dp, 0)
        g_row = over("grow_ov", [1, 2048], BF16, sc3, 0)
        wdw_nat = over("wdwnat_ov", [128, 8192], BF16, sc1, 0)
        wst = over("wst_ov", [128, 8192], F32, Wout_sb, 0)

        ps_big = es.enter_context(nc.psum_tensor("ps_big", [128, 2048], F32))
        ps_tp1 = es.enter_context(nc.psum_tensor("ps_tp1", [128, 1024], BF16))
        ps_tp2 = es.enter_context(nc.psum_tensor("ps_tp2", [128, 1024], BF16))
        ps_zd = es.enter_context(nc.psum_tensor("ps_zd", [128, 1024], F32))

        sem = {}
        for sname in ("dw", "dc", "dx0", "dx1", "so0", "so1", "pe", "ac", "dv",
                      "wdw", "wtw", "win", "wo"):
            sem[sname] = es.enter_context(nc.semaphore("s_" + sname))
        S = Sched()
        M = {}

        DW, DC = sem["dw"], sem["dc"]
        PE, AC, DV = sem["pe"], sem["ac"], sem["dv"]
        WDW, WTW, WIN, WO = sem["wdw"], sem["wtw"], sem["win"], sem["wo"]

        const_loads = [
            (Ib_sb, cIb_d), (If_sb, cIf_d),
            (Ob_sb, cOb_d), (Of_sb, cOf_d), (O16_sb, cO16_d),
            (bp_nat, bp_d.rearrange("(s p) j -> p s j", p=128)),
            (Cp_sb, cp_d), (scale_sb, scale_d),
        ]
        for dst, src in const_loads:
            S.op("sync", (lambda e, d=dst, s=src: e.dma_start(out=d[:, :] if d.shape[0] != 16 or True else d, in_=s)),
                 inc=DC, n=16)
        M["dc_all"] = 16 * len(const_loads)

        gq = []
        ndw = [0]

        def gdma(dst_ap, src_ap, waits=(), gsem=None):
            gsem = DW if gsem is None else gsem
            v = S.op("gpsimd", (lambda e, d=dst_ap, s=src_ap: e.dma_start(out=d, in_=s)),
                     waits=waits, inc=gsem, n=16)
            ndw[0] += 1
            return v

        M["dw_smalls"] = None
        gdma(T_sb[:, :], cT_d)
        gdma(E_sb[:, :], cE_d)
        gdma(bin_row[:, :], bin_d)
        gdma(bout_row[:, :], bout_d)
        gdma(wdb_row[:, :], wdb_d)
        gdma(Dp_row[:, :], dp_d)
        gdma(wtb_nat[:, :], wtb_d)
        gdma(wg_nat[:, :].rearrange("p (k e) -> p k e", k=8),
             wg_d.rearrange("(k p) e -> p k e", p=128))
        gdma(a_nat[:, :].rearrange("p (e r) -> p e r", e=6),
             a_d.rearrange("e l r -> l e r"))
        M["dw_smalls"] = gdma(B_all[:, :].rearrange("r (e c) -> r e c", e=6),
                              b_d.rearrange("e r c -> r e c"))
        M["dw_wdw"] = None

        def f_bptp(e):
            r = None
            for s in range(16):
                r = e.transpose(ps_big[0:16, s * 128:(s + 1) * 128],
                                bp_nat[:, s * 16:(s + 1) * 16], If_sb[:, :])
            return r
        M["pe_bptp"] = S.op("tensor", f_bptp, waits=[(DC, M["dc_all"])], inc=PE)

        M["ac_bpT"] = S.op("scalar",
                           lambda e: e.activation(BpT_sb[:, :], ps_big[0:16, :], AF.Identity, bias=0.0),
                           waits=[(PE, M["pe_bptp"])], inc=AC)

        def f_prod(e, h):
            return e.tensor_tensor(prod_h[:, :], BpT_sb[:, h * 1024:(h + 1) * 1024],
                                   Cp_sb[:, h * 1024:(h + 1) * 1024], ALU.mult)
        def f_grow(e, h):
            r = None
            for n in range(2):
                r = e.matmul(ps_big[0:1, (h * 2 + n) * 512:(h * 2 + n + 1) * 512],
                             O16_sb[:, :], prod_h[0:16, n * 512:(n + 1) * 512],
                             start=True, stop=True)
            return r
        M["dv_prod0"] = S.op("vector", lambda e: f_prod(e, 0),
                             waits=[(AC, M["ac_bpT"]), (DC, M["dc_all"])], inc=DV)
        M["pe_grow0"] = S.op("tensor", lambda e: f_grow(e, 0),
                             waits=[(DV, M["dv_prod0"])], inc=PE)
        M["dv_prod1"] = S.op("vector", lambda e: f_prod(e, 1),
                             waits=[(PE, M["pe_grow0"])], inc=DV)
        M["pe_grow1"] = S.op("tensor", lambda e: f_grow(e, 1),
                             waits=[(DV, M["dv_prod1"])], inc=PE)
        M["ac_grow"] = S.op("scalar",
                            lambda e: e.activation(g_row[:, :], ps_big[0:1, :], AF.Copy, bias=0.0),
                            waits=[(PE, M["pe_grow1"])], inc=AC)

        def f_bcast(e, row, n0):
            return e.matmul(ps_big[:, n0 * 512:(n0 + 1) * 512], Ob_sb[:, :],
                            row[0:1, n0 * 512:(n0 + 1) * 512], start=True, stop=True)
        def f_bc_all(e, row):
            r = None
            for n in range(4):
                r = f_bcast(e, row, n)
            return r
        M["pe_gbc"] = S.op("tensor", lambda e: f_bc_all(e, g_row),
                           waits=[(AC, M["ac_grow"])], inc=PE)
        M["ac_gb"] = S.op("scalar",
                          lambda e: e.activation(gb[:, :], ps_big[:, :], AF.Copy, bias=0.0),
                          waits=[(PE, M["pe_gbc"])], inc=AC)
        M["pe_dbc"] = S.op("tensor", lambda e: f_bc_all(e, Dp_row),
                           waits=[(AC, M["ac_gb"]), (DW, M["dw_smalls"])], inc=PE)
        M["ac_dpb"] = S.op("scalar",
                           lambda e: e.activation(dpb[:, :], ps_big[:, :], AF.Copy, bias=0.0),
                           waits=[(PE, M["pe_dbc"])], inc=AC)
        M["pe_sbc"] = S.op("tensor",
                           lambda e: e.matmul(ps_big[:, 0:1], Of_sb[:, :], scale_sb[:, :],
                                              start=True, stop=True),
                           waits=[(AC, M["ac_dpb"])], inc=PE)
        M["ac_svec"] = S.op("scalar",
                            lambda e: e.activation(scale_vec[:, :], ps_big[:, 0:1], AF.Identity, bias=0.0),
                            waits=[(PE, M["pe_sbc"])], inc=AC)

        M["pe_wtbtp"] = S.op("tensor",
                             lambda e: e.transpose(ps_tp1[:, 0:8], wtb_nat[0:8, :], Ib_sb[0:8, 0:8]),
                             waits=[(DW, M["dw_smalls"]), (DC, M["dc_all"])], inc=PE)
        M["dv_wtb"] = S.op("vector",
                           lambda e: e.tensor_copy(wtb_col[:, :], ps_tp1[:, 0:8]),
                           waits=[(PE, M["pe_wtbtp"])], inc=DV)

        def f_atp(e):
            r = None
            for ex in range(6):
                r = e.transpose(ps_tp2[0:16, ex * 100:(ex + 1) * 100],
                                a_nat[0:100, ex * 16:(ex + 1) * 16], Ib_sb[0:100, 0:100])
            return r
        M["pe_atp"] = S.op("tensor", f_atp, waits=[(DW, M["dw_smalls"])], inc=PE)
        M["ac_aT"] = S.op("scalar",
                          lambda e: e.activation(aT_sb[:, :], ps_tp2[0:16, 0:600], AF.Copy, bias=0.0),
                          waits=[(PE, M["pe_atp"])], inc=AC)

        for kc in range(8):
            def f_tokmm(e, kc=kc):
                r = None
                for ex in range(6):
                    slot = ex * 100 if ex < 5 else 512
                    r = e.matmul(ps_zd[:, slot:slot + 100],
                                 B_all[0:16, ex * 1024 + kc * 128: ex * 1024 + (kc + 1) * 128],
                                 aT_sb[0:16, ex * 100:(ex + 1) * 100],
                                 start=True, stop=True)
                return r
            wl = [(AC, M["ac_aT"]), (DW, M["dw_smalls"])]
            if kc > 0:
                wl.append((DV, M[f"dv_tok{kc-1}"]))
            M[f"pe_tok{kc}"] = S.op("tensor", f_tokmm, waits=wl, inc=PE)

            def f_tokev(e, kc=kc):
                e.tensor_copy(TG[:, kc * 606: kc * 606 + 500], ps_zd[:, 0:500])
                return e.tensor_copy(TG[:, kc * 606 + 500: kc * 606 + 600], ps_zd[:, 512:612])
            M[f"dv_tok{kc}"] = S.op("vector", f_tokev,
                                    waits=[(PE, M[f"pe_tok{kc}"])], inc=DV)
        M["pe_tokdone"] = M["pe_tok7"]
        M["dv_tokdone"] = M["dv_tok7"]

        M["dv_wg"] = S.op("vector",
                          lambda e: e.tensor_copy(
                              TG[:, :].rearrange("p (k c) -> p k c", k=8)[:, :, 600:606],
                              wg_nat[:, :].rearrange("p (k e) -> p k e", k=8)),
                          waits=[(DW, M["dw_smalls"]), (DV, M["dv_tokdone"])], inc=DV)

        S.op("sync", lambda e: e.dma_start(
                 out=wst[:, :].rearrange("p (k d) -> p k d", k=8),
                 in_=wdw_d.rearrange("(k p) d -> p k d", p=128)),
             inc=WDW, n=16)
        def f_wdwc(e):
            e.activation(wdw_nat[:, 0:4096], wst[:, 0:4096], AF.Copy, bias=0.0)
            return e.activation(wdw_nat[:, 4096:8192], wst[:, 4096:8192],
                                AF.Copy, bias=0.0)
        M["ac_wdwc"] = S.op("scalar", f_wdwc,
                            waits=[(WDW, 16), (DV, M["dv_prod1"])],
                            inc=AC)
        S.op("sync", lambda e: e.dma_start(
                 out=wst[:, :].rearrange("p (k d) -> p k d", k=8),
                 in_=wtw_d.rearrange("(k p) d -> p k d", p=128)),
             waits=[(AC, M["ac_wdwc"])], inc=WTW, n=16)
        def f_wtwc(e):
            e.activation(wtw_nat[:, 0:4096], wst[:, 0:4096], AF.Copy, bias=0.0)
            return e.activation(wtw_nat[:, 4096:8192], wst[:, 4096:8192],
                                AF.Copy, bias=0.0)
        M["ac_wtwc"] = S.op("scalar", f_wtwc,
                            waits=[(WTW, 16), (PE, M["pe_dbc"])],
                            inc=AC)
        for k in range(8):
            gdma(Win_sb[:, k * 2048:(k + 1) * 2048],
                 win_d[k * 128:(k + 1) * 128, :], gsem=WIN)
        M["dw_win"] = S.val(WIN)
        for k in range(16):
            gdma(Wout_sb[:, k * 1024:(k + 1) * 1024],
                 wout_d[k * 128:(k + 1) * 128, :],
                 waits=([(AC, M["ac_wtwc"])] if k == 0 else ()), gsem=WO)
        M["dw_wout"] = S.val(WO)

        for rj in range(8):
            def f_wdtp(e, rj=rj):
                r = None
                for k in range(8):
                    r = e.transpose(ps_tp1[:, k * 128:(k + 1) * 128],
                                    wdw_nat[:, k * 1024 + rj * 128: k * 1024 + (rj + 1) * 128],
                                    Ib_sb[:, :])
                return r
            wl = [(AC, M["ac_wdwc"]), (DV, M["dv_wtb"])]
            if rj > 0:
                wl.append((DV, M[f"dv_wdtp{rj-1}"]))
            M[f"pe_wdtp{rj}"] = S.op("tensor", f_wdtp, waits=wl, inc=PE)
            M[f"dv_wdtp{rj}"] = S.op("vector",
                                     lambda e, rj=rj: e.tensor_copy(
                                         wdw_dmaj[:, rj * 1024:(rj + 1) * 1024], ps_tp1[:, :]),
                                     waits=[(PE, M[f"pe_wdtp{rj}"])], inc=DV)
        M["pe_wdtp_done"] = M["pe_wdtp7"]
        M["dv_wdtp_done"] = M["dv_wdtp7"]

        w2t_seq = [(kcc, n) for kcc in range(8) for n in range(2)]
        for idx, (kcc, n) in enumerate(w2t_seq):
            bank = idx % 4

            def f_w2t(e, kcc=kcc, n=n, bank=bank):
                r = None
                for kd in range(8):
                    r = e.matmul(ps_big[:, bank * 512:(bank + 1) * 512],
                                 wtw_nat[:, kd * 1024 + kcc * 128: kd * 1024 + (kcc + 1) * 128],
                                 wdw_dmaj[:, kd * 1024 + n * 512: kd * 1024 + (n + 1) * 512],
                                 start=(kd == 0), stop=(kd == 7))
                return r
            wl = [(AC, M["ac_wtwc"]), (DV, M["dv_wdtp_done"]), (AC, M["ac_svec"])]
            if idx >= 4:
                wl.append((DV, M[f"dv_w2t{idx-4}"]))
            M[f"pe_w2t{idx}"] = S.op("tensor", f_w2t, waits=wl, inc=PE)
            M[f"dv_w2t{idx}"] = S.op("vector",
                                     lambda e, kcc=kcc, n=n, bank=bank: e.tensor_copy(
                                         W2T[:, kcc * 1024 + n * 512: kcc * 1024 + (n + 1) * 512],
                                         ps_big[:, bank * 512:(bank + 1) * 512]),
                                     waits=[(PE, M[f"pe_w2t{idx}"])], inc=DV)
        M["pe_w2t_done"] = M["pe_w2t15"]
        M["dv_w2t_done"] = M["dv_w2t15"]

        M["ac_sq0"] = S.op("scalar",
                           lambda e: e.activation(ps_zd[:, :], x_buf0[:, :], AF.Square),
                           waits=[(sem["dx0"], 16), (DV, M["dv_tokdone"])], inc=AC)
        def f_stats0(e):
            e.reduce_sum(stat[:, 0:1], x_buf0[:, :], axis=AX.X)
            return e.reduce_sum(stat[:, 1:2], ps_zd[:, :], axis=AX.X)
        M["dv_red0f"] = S.op("vector", f_stats0, waits=[(AC, M["ac_sq0"])], inc=DV)
        def f_m10(e):
            e.tensor_scalar(stat[:, 2:3], stat[:, 0:1], 1.0 / 1024.0, None, op0=ALU.mult)
            return e.tensor_scalar(stat[:, 3:4], stat[:, 1:2], 1.0 / 1024.0, None, op0=ALU.mult)
        M["dv_mv10"] = S.op("vector", f_m10, waits=[(DV, M["dv_red0f"])], inc=DV)
        M["dv_mv20"] = S.op("vector",
                            lambda e: e.tensor_tensor(stat[:, 4:5], stat[:, 2:3],
                                                      stat[:, 2:3], ALU.mult),
                            waits=[(DV, M["dv_mv10"])], inc=DV)
        M["dv_mv0f"] = S.op("vector",
                            lambda e: e.tensor_tensor(stat[:, 3:4], stat[:, 3:4],
                                                      stat[:, 4:5], ALU.subtract),
                            waits=[(DV, M["dv_mv20"])], inc=DV)
        M["ac_lnv0"] = S.op("scalar",
                            lambda e: e.activation(stat[:, 5:6], stat[:, 3:4], AF.Ln,
                                                   bias=np.float32(1e-5)),
                            waits=[(DV, M["dv_mv0f"])], inc=AC)
        M["ac_rstd0"] = S.op("scalar",
                             lambda e: e.activation(stat[:, 6:7], stat[:, 5:6], AF.Exp,
                                                    scale=np.float32(-0.5)),
                             waits=[(AC, M["ac_lnv0"])], inc=AC)
        M["dv_xn0"] = S.op("vector",
                           lambda e: e.tensor_scalar(
                               xn_tm0[:, :], x_buf0[:, :], stat[:, 2:3], stat[:, 6:7],
                               op0=ALU.subtract, op1=ALU.mult),
                           waits=[(AC, M["ac_rstd0"]), (PE, M["pe_grow1"])], inc=DV)

        vw_seq = [(ex, n) for ex in range(6) for n in range(2)]
        for idx, (ex, n) in enumerate(vw_seq):
            bank = idx % 4

            def f_vw(e, ex=ex, n=n, bank=bank):
                r = None
                for kcc in range(8):
                    r = e.matmul(ps_big[0:100, bank * 512:(bank + 1) * 512],
                                 TG[:, kcc * 606 + ex * 100: kcc * 606 + (ex + 1) * 100],
                                 W2T[:, kcc * 1024 + n * 512: kcc * 1024 + (n + 1) * 512],
                                 start=(kcc == 0), stop=(kcc == 7))
                return r
            wl = [(DV, M["dv_w2t_done"]), (DV, M["dv_tokdone"])]
            if idx >= 4:
                wl.append((AC, M[f"ac_vw{idx-4}"]))
            M[f"pe_vw{idx}"] = S.op("tensor", f_vw, waits=wl, inc=PE)
            M[f"ac_vw{idx}"] = S.op("scalar",
                                    lambda e, ex=ex, n=n, bank=bank: e.activation(
                                        VW[0:100, ex * 1024 + n * 512: ex * 1024 + (n + 1) * 512],
                                        ps_big[0:100, bank * 512:(bank + 1) * 512],
                                        AF.Copy, bias=0.0),
                                    waits=[(PE, M[f"pe_vw{idx}"])], inc=AC)
        M["pe_vw_done"] = M["pe_vw11"]
        M["ac_vw_done"] = M["ac_vw11"]

        def f_b2(e):
            r = None
            for n in range(2):
                for kd in range(8):
                    r = e.matmul(ps_zd[0:1, n * 512:(n + 1) * 512],
                                 wtb_col[:, kd:kd + 1],
                                 wdw_dmaj[:, kd * 1024 + n * 512: kd * 1024 + (n + 1) * 512],
                                 start=(kd == 0), stop=(kd == 7))
            return r
        M["pe_b2"] = S.op("tensor", f_b2,
                          waits=[(AC, M["ac_vw_done"]), (DV, M["dv_wtb"]),
                                 (DV, M["dv_red0f"])], inc=PE)
        M["ac_b2"] = S.op("scalar",
                          lambda e: e.activation(B2_row[:, :], ps_zd[0:1, :], AF.Copy, bias=0.0),
                          waits=[(PE, M["pe_b2"])], inc=AC)

        def f_b2vw(e):
            r = None
            for ex in range(6):
                r = e.activation(VW[0:1, ex * 1024:(ex + 1) * 1024], B2_row[:, :],
                                 AF.Copy, bias=0.0)
            return r
        M["ac_b2vw"] = S.op("scalar", f_b2vw, waits=[(AC, M["ac_b2"])], inc=AC)

        M["dv_ct0"] = S.op("vector", lambda e: e.memset(carry_t[:, :], 0.0), inc=DV)

        DX = [sem["dx0"], sem["dx1"]]
        SO = [sem["so0"], sem["so1"]]
        nx = [0, 0]
        nso = [0, 0]

        def x_load(c, waits):
            p = c % 2
            buf = x_buf0 if p == 0 else x_buf1
            S.op("sync", (lambda e, b=buf, c=c: e.dma_start(out=b[:, :],
                                                            in_=x_d[c * 128:(c + 1) * 128, :])),
                 waits=waits, inc=DX[p], n=16)
            nx[p] += 1
            return 16 * nx[p]

        M["x0"] = x_load(0, [(PE, M["pe_tokdone"])])
        M["x1"] = x_load(1, [(PE, M["pe_tokdone"])])

        v_final = {}
        _chopn = [0]
        _trunc = [False]

        def cop(*a, **kw):
            _chopn[0] += 1
            if _chopn[0] > chop:
                _trunc[0] = True
                return None
            return S.op(*a, **kw)
        def _bufs(c):
            p = c % 2
            return p, (x_buf0 if p == 0 else x_buf1), (out_buf0 if p == 0 else out_buf1)

        def _xn(c):
            p = c % 2
            return (xn_tm0 if p == 0 else xn_tm1), (xn_fm0 if p == 0 else xn_fm1)

        def emit_front(c):
            if f"dv_xn{c}" in M:
                return
            p, xb, ob = _bufs(c)

            M[f"ac_sq{c}"] = cop("scalar",
                                  lambda e, xb=xb: e.activation(scr4[:, :], xb[:, :], AF.Square),
                                  waits=[(DX[p], 16 * (c // 2 + 1))] +
                                        ([(PE, M["pe_w2t_done"])] if c == 0 else []) +
                                        ([(DV, v_final[c - 2])] if c >= 2 else []),
                                  inc=AC)

            def f_stats(e, xb=xb):
                e.reduce_sum(stat[:, 0:1], xb[:, :], axis=AX.X)
                return e.reduce_sum(stat[:, 1:2], scr4[:, :], axis=AX.X)
            M[f"dv_red{c}"] = cop("vector", f_stats,
                                   waits=[(AC, M[f"ac_sq{c}"]),
                                          (DX[p], 16 * (c // 2 + 1))], inc=DV)

            def f_m1(e):
                e.tensor_scalar(stat[:, 2:3], stat[:, 0:1], 1.0 / 1024.0, None, op0=ALU.mult)
                return e.tensor_scalar(stat[:, 3:4], stat[:, 1:2], 1.0 / 1024.0, None, op0=ALU.mult)
            M[f"dv_mv1{c}"] = cop("vector", f_m1, waits=[(DV, M[f"dv_red{c}"])], inc=DV)
            M[f"dv_mv2{c}"] = cop("vector",
                                  lambda e: e.tensor_tensor(stat[:, 4:5], stat[:, 2:3],
                                                            stat[:, 2:3], ALU.mult),
                                  waits=[(DV, M[f"dv_mv1{c}"])], inc=DV)
            M[f"dv_mv{c}"] = cop("vector",
                                 lambda e: e.tensor_tensor(stat[:, 3:4], stat[:, 3:4],
                                                           stat[:, 4:5], ALU.subtract),
                                 waits=[(DV, M[f"dv_mv2{c}"])], inc=DV)

            M[f"ac_lnv{c}"] = cop("scalar",
                                  lambda e: e.activation(stat[:, 5:6], stat[:, 3:4], AF.Ln,
                                                         bias=np.float32(1e-5)),
                                  waits=[(DV, M[f"dv_mv{c}"])], inc=AC)
            M[f"ac_rstd{c}"] = cop("scalar",
                                   lambda e: e.activation(stat[:, 6:7], stat[:, 5:6], AF.Exp,
                                                          scale=np.float32(-0.5)),
                                   waits=[(AC, M[f"ac_lnv{c}"])], inc=AC)

            xntm, xnfm = _xn(c)
            M[f"dv_xn{c}"] = cop("vector",
                                  (lambda e, xb=xb, xntm=xntm: e.tensor_scalar(
                                      xntm[:, :], xb[:, :], stat[:, 2:3], stat[:, 6:7],
                                      op0=ALU.subtract, op1=ALU.mult)),
                                  waits=[(AC, M[f"ac_rstd{c}"])] +
                                        ([(PE, M["pe_grow1"])] if c == 0 else []),
                                  inc=DV)

        def emit_body(c):
            p, xb, ob = _bufs(c)
            xn_tm, xn_fm = _xn(c)
            for h in range(2):
                def f_m1(e, h=h):
                    r = None
                    for kc in range(8):
                        lhs = xn_fm[:, kc * 128:(kc + 1) * 128]
                        for n in range(2 * h, 2 * h + 2):
                            e.matmul(ps_big[:, n * 512:(n + 1) * 512], lhs,
                                     Win_sb[:, kc * 2048 + n * 512: kc * 2048 + (n + 1) * 512],
                                     start=(kc == 0), stop=False)
                    for n in range(2 * h, 2 * h + 2):
                        r = e.matmul(ps_big[:, n * 512:(n + 1) * 512], Ob_sb[:, :],
                                     bin_row[0:1, n * 512:(n + 1) * 512],
                                     start=False, stop=True)
                    return r
                wl = [(DV, M[f"dv_xnfm{c}"]), (WIN, M["dw_win"])]
                if h == 0:
                    if c > 0:
                        wl.append((AC, M[f"ac_yev{c-1}"]))
                    else:
                        wl.append((AC, M["ac_vw9"]))
                elif c == 0:
                    wl.append((AC, M["ac_vw11"]))
                M[f"pe_m1{c}_{h}"] = cop("tensor", f_m1, waits=wl, inc=PE)
            M[f"pe_m1{c}"] = M[f"pe_m1{c}_1"]

            def f_z(e):
                r = None
                for kc in range(8):
                    lhs = xn_fm[:, kc * 128:(kc + 1) * 128]
                    e.matmul(ps_zd[:, 0:300], lhs,
                             TG[:, kc * 606: kc * 606 + 300],
                             start=(kc == 0), stop=(kc == 7))
                    r = e.matmul(ps_zd[:, 512:818], lhs,
                                 TG[:, kc * 606 + 300: kc * 606 + 606],
                                 start=(kc == 0), stop=(kc == 7))
                return r
            wl = [(DV, M[f"dv_xnfm{c}"])]
            if c > 0:
                wl.append((DV, v_final[c - 1]))
            else:
                wl.append((AC, M["ac_b2vw"]))
                wl.append((DV, M["dv_wg"]))
            M[f"pe_z{c}"] = cop("tensor", f_z, waits=wl, inc=PE)

            def emit_silu(h):
                H = slice(h * 1024, (h + 1) * 1024)
                wl = [(PE, M[f"pe_m1{c}_{h}"])]
                if c == 0:
                    wl.append((DV, M["dv_w2t_done"]))
                elif h == 0:
                    wl.append((DV, M[f"dv_fa{c-1}"]))
                M[f"ac_sigA{c}_{h}"] = cop("scalar",
                                      (lambda e, H=H: e.activation(sc1[:, H], ps_big[:, H],
                                                                   AF.Exp, scale=np.float32(-1.0))),
                                      waits=wl, inc=AC)
                M[f"ac_sigB{c}_{h}"] = cop("scalar",
                                      (lambda e, H=H: e.activation(sc2[:, H], sc1[:, H],
                                                                   AF.Ln, bias=np.float32(1.0))),
                                      waits=[(AC, M[f"ac_sigA{c}_{h}"])], inc=AC)
                M[f"ac_sig{c}_{h}"] = cop("scalar",
                                     (lambda e, H=H: e.activation(sc1[:, H], sc2[:, H],
                                                                  AF.Exp, scale=np.float32(-1.0))),
                                     waits=[(AC, M[f"ac_sigB{c}_{h}"])], inc=AC)
                wl = [(AC, M[f"ac_sig{c}_{h}"])]
                if c == 0 and h == 0:
                    wl.append((PE, M["pe_w2t_done"]))
                M[f"dv_u{c}_{h}"] = cop("vector",
                                    (lambda e, H=H: e.tensor_tensor(u_sb[:, H], ps_big[:, H],
                                                                    sc1[:, H], ALU.mult)),
                                    waits=wl, inc=DV)
                M[f"dv_presg{c}_{h}"] = cop("vector",
                                    (lambda e, H=H: e.tensor_tensor(xi_g[:, H], u_sb[:, H],
                                                                    gb[:, H], ALU.mult)),
                                    waits=[(DV, M[f"dv_u{c}_{h}"])] +
                                          ([(AC, M["ac_dpb"])] if c == 0 and h == 0 else []),
                                    inc=DV)

            emit_silu(0)

            M[f"dv_m6{c}"] = cop("vector",
                                 lambda e: e.reduce_max(m6[:, :], ps_zd[:, 812:818], axis=AX.X),
                                 waits=[(PE, M[f"pe_z{c}"])], inc=DV)
            prev = M[f"dv_m6{c}"]
            def st1(e):
                e.tensor_scalar(eq6[:, :], ps_zd[:, 812:818], m6[:, 0:1], None, op0=ALU.is_equal)
                return e.memset(ec6[:, 0:1], 0.0)
            prev = cop("vector", st1, waits=[(DV, prev)], inc=DV)
            for ex in range(1, 6):
                prev = cop("vector",
                           (lambda e, ex=ex: e.tensor_tensor(ec6[:, ex:ex + 1], ec6[:, ex - 1:ex],
                                                             eq6[:, ex - 1:ex], ALU.add)),
                           waits=[(DV, prev)], inc=DV)
            prev = cop("vector",
                       lambda e: e.tensor_scalar(ecz6[:, :], ec6[:, :], 0.0, None, op0=ALU.is_equal),
                       waits=[(DV, prev)], inc=DV)
            M[f"dv_gate6{c}"] = cop("vector",
                       lambda e: e.tensor_tensor(gates[:, :], eq6[:, :], ecz6[:, :], ALU.mult),
                       waits=[(DV, prev)], inc=DV)

            emit_silu(1)
            M[f"ac_sig{c}"] = M[f"ac_sig{c}_1"]

            def f_exps(e):
                r = None
                for ex in range(6):
                    off = ex * 100 if ex < 3 else 512 + (ex - 3) * 100
                    r = e.activation(expz[:, ex * 100:(ex + 1) * 100],
                                     ps_zd[:, off:off + 100], AF.Exp,
                                     scale=np.float32(1.0 / 32.0),
                                     accum_out=sums[:, ex:ex + 1])
                return r
            wl = [(PE, M[f"pe_z{c}"])]
            if c == 0:
                wl.append((AC, M["ac_aT"]))
                wl.append((AC, M["ac_b2vw"]))
            M[f"ac_exps{c}"] = cop("scalar", f_exps, waits=wl, inc=AC)


            prev = cop("vector",
                       lambda e: e.reciprocal(rsum[:, :], sums[:, :]),
                       waits=[(DV, M[f"dv_gate6{c}"]), (AC, M[f"ac_exps{c}"])], inc=DV)
            M[f"dv_gsc{c}"] = cop("vector",
                                  lambda e: e.tensor_tensor(gsc[:, :], gates[:, :], rsum[:, :], ALU.mult),
                                  waits=[(DV, prev)], inc=DV)

            def f_wbuild(e):
                r = None
                for ex in range(6):
                    r = e.tensor_scalar(w_sb[:, ex * 100:(ex + 1) * 100],
                                        expz[:, ex * 100:(ex + 1) * 100],
                                        gsc[:, ex:ex + 1], None, op0=ALU.mult)
                return r
            M[f"dv_w0{c}"] = cop("vector", f_wbuild, waits=[(DV, M[f"dv_gsc{c}"])], inc=DV)
            M[f"dv_w1{c}"] = cop("vector",
                                 lambda e: e.tensor_tensor(tmp6[:, :], gates[:, :],
                                                           w_sb[:, 0:600:100], ALU.subtract),
                                 waits=[(DV, M[f"dv_w0{c}"])], inc=DV)
            M[f"dv_w{c}"] = cop("vector",
                                lambda e: e.tensor_copy(w_sb[:, 0:600:100], tmp6[:, :]),
                                waits=[(DV, M[f"dv_w1{c}"])], inc=DV)

            for h in range(2):
                H = slice(h * 1024, (h + 1) * 1024)
                wl = [(DV, M[f"dv_u{c}_{h}"])]
                if c == 0 and h == 0:
                    wl.append((AC, M["ac_dpb"]))
                M[f"dv_pres{c}_{h}"] = cop("vector",
                                    (lambda e, H=H: e.tensor_tensor(xi_dp[:, H], u_sb[:, H],
                                                                    dpb[:, H], ALU.mult)),
                                    waits=wl, inc=DV)
            M[f"dv_pres{c}"] = M[f"dv_pres{c}_1"]

            for h in range(2):
                def f_corr(e, c=c, h=h):
                    r = None
                    for b in range(2 * h, 2 * h + 2):
                        r = e.matmul(ps_big[:, b * 512:(b + 1) * 512], carry_t[:, :],
                                     E_sb[:, b * 512:(b + 1) * 512], start=True, stop=False)
                    return r
                wl = [(DV, M[f"dv_u{c}_{h}"])]
                if h == 0:
                    if c > 0:
                        wl.append((AC, M[f"ac_ct{c-1}"]))
                    else:
                        wl.append((DV, M["dv_ct0"]))
                M[f"pe_corr{c}_{h}"] = cop("tensor", f_corr, waits=wl, inc=PE)
                def f_tmm(e, c=c, h=h):
                    r = None
                    for s in range(8 * h, 8 * h + 8):
                        r = e.matmul(ps_big[:, s * 128:(s + 1) * 128],
                                     xi_g[:, s * 128:(s + 1) * 128], T_sb[:, :],
                                     start=False, stop=(s % 4 == 3))
                    return r
                M[f"pe_scan{c}_{h}"] = cop("tensor", f_tmm,
                                           waits=[(DV, M[f"dv_presg{c}_{h}"])], inc=PE)
            M[f"pe_scan{c}"] = M[f"pe_scan{c}_1"]

            M[f"ac_ext{c}"] = cop("scalar",
                                   lambda e: e.activation(carry_col[:, :], ps_big[:, 127:2048:128],
                                                          AF.Copy, bias=0.0),
                                   waits=[(PE, M[f"pe_scan{c}"])], inc=AC)

            M[f"pe_ctp{c}"] = cop("tensor",
                                   lambda e: e.transpose(ps_tp1[0:16, 896:1024],
                                                         carry_col[:, :], Ib_sb[:, :]),
                                   waits=[(AC, M[f"ac_ext{c}"])], inc=PE)
            M[f"ac_ct{c}"] = cop("scalar",
                                  lambda e: e.activation(carry_t[:, :], ps_tp1[0:16, 896:1024],
                                                         AF.Copy, bias=0.0),
                                  waits=[(PE, M[f"pe_ctp{c}"])], inc=AC)

            def f_wtp(e):
                r = None
                for ex in range(6):
                    r = e.transpose(ps_tp2[0:100, ex * 128:(ex + 1) * 128],
                                    w_sb[:, ex * 100:(ex + 1) * 100], Ib_sb[:, :])
                return r
            wl = [(DV, M[f"dv_w{c}"])]
            if c > 0:
                wl.append((AC, M[f"ac_wt{c-1}"]))
            else:
                wl.append((AC, M["ac_aT"]))
            M[f"pe_wtp{c}"] = cop("tensor", f_wtp, waits=wl, inc=PE)

            M[f"ac_wt{c}"] = cop("scalar",
                                  lambda e: e.activation(w_t[0:100, :], ps_tp2[0:100, 0:768],
                                                         AF.Copy, bias=0.0),
                                  waits=[(PE, M[f"pe_wtp{c}"])], inc=AC)

            def f_imm(e):
                r = None
                for s in range(NSL):
                    r = e.matmul(ps_big[:, s * 128:(s + 1) * 128],
                                 xi_dp[:, s * 128:(s + 1) * 128], Ib_sb[:, :],
                                 start=False, stop=True, skip_group_check=True)
                return r
            M[f"pe_imm{c}"] = cop("tensor", f_imm,
                                   waits=[(AC, M[f"ac_ext{c}"]), (DV, M[f"dv_pres{c}"])], inc=PE)

            M[f"ac_yev{c}"] = cop("scalar",
                                   lambda e: e.activation(u_sb[:, :], ps_big[:, :],
                                                          AF.Copy, bias=0.0),
                                   waits=[(PE, M[f"pe_imm{c}"]), (DV, M[f"dv_pres{c}"])], inc=AC)


        def emit_body_c(c):
            p, xb, ob = _bufs(c)
            xn_tm, xn_fm = _xn(c)
            def f_delta(e):
                r = None
                for kc in range(8):
                    lhs = xn_fm[:, kc * 128:(kc + 1) * 128]
                    for n in range(2):
                        r = e.matmul(ps_zd[:, n * 512:(n + 1) * 512], lhs,
                                     wdw_dmaj[:, kc * 1024 + n * 512: kc * 1024 + (n + 1) * 512],
                                     start=(kc == 0), stop=False)
                for ex in range(6):
                    lhs = w_t[0:100, ex * 128:(ex + 1) * 128]
                    for n in range(2):
                        r = e.matmul(ps_zd[:, n * 512:(n + 1) * 512], lhs,
                                     VW[0:100, ex * 1024 + n * 512: ex * 1024 + (n + 1) * 512],
                                     start=False, stop=False)
                for n in range(2):
                    r = e.matmul(ps_zd[:, n * 512:(n + 1) * 512], Ob_sb[:, :],
                                 wdb_row[0:1, n * 512:(n + 1) * 512],
                                 start=False, stop=True)
                return r
            wl = [(AC, M[f"ac_wt{c}"]), (AC, M[f"ac_exps{c}"]), (DV, M[f"dv_gsc{c}"])]
            if c == 0:
                wl.append((AC, M["ac_b2vw"]))
            M[f"pe_delta{c}"] = cop("tensor", f_delta, waits=wl, inc=PE)

            M[f"ac_cb{c}"] = cop("scalar",
                                  lambda e: e.activation(sc3[:, :], ps_zd[:, :], AF.Exp,
                                                         scale=scale_vec[:, 0:1]),
                                  waits=[(PE, M[f"pe_delta{c}"]), (AC, M["ac_svec"])], inc=AC)

            M[f"dv_epsA{c}"] = cop("vector",
                                   lambda e: e.tensor_scalar(sc2[:, 1024:2048], sc3[:, :], 0.0,
                                                             EPS32, op0=ALU.is_equal, op1=ALU.mult),
                                   waits=[(AC, M[f"ac_cb{c}"])], inc=DV)
            M[f"dv_eps{c}"] = cop("vector",
                                  lambda e: e.tensor_tensor(sc2[:, 0:1024], sc3[:, :],
                                                            sc2[:, 1024:2048], ALU.add),
                                  waits=[(DV, M[f"dv_epsA{c}"])], inc=DV)

            M[f"ac_ln{c}"] = cop("scalar",
                                  lambda e: e.activation(sc1[:, 0:1024], sc2[:, 0:1024], AF.Ln),
                                  waits=[(DV, M[f"dv_eps{c}"])], inc=AC)

            M[f"dv_ps{c}"] = cop("vector",
                                  lambda e: e.tensor_tensor(sc2[:, 0:1024], sc1[:, 0:1024],
                                                            xn_tm[:, :], ALU.add),
                                  waits=[(AC, M[f"ac_ln{c}"])], inc=DV)


        def emit_front_b(c):
            p, xb, ob = _bufs(c)
            xn_tm, xn_fm = _xn(c)
            def f_xntp(e):
                r = None
                for k in range(8):
                    r = e.transpose(ps_tp1[:, k * 128:(k + 1) * 128],
                                    xn_tm[:, k * 128:(k + 1) * 128], Ib_sb[:, :])
                return r
            wl = [(DV, M[f"dv_xn{c}"])]
            if c > 0:
                wl.append((DV, M[f"dv_xnfm{c-1}"]))
                wl.append((AC, M[f"ac_ct{c-1}"]))
            else:
                wl.append((DV, M["dv_wdtp_done"]))
            M[f"pe_xntp{c}"] = cop("tensor", f_xntp, waits=wl, inc=PE)

            M[f"dv_xnfm{c}"] = cop("vector",
                                    lambda e: e.tensor_copy(xn_fm[:, :], ps_tp1[:, :]),
                                    waits=[(PE, M[f"pe_xntp{c}"])], inc=DV)



        def emit_tail_a(c):
            p, xb, ob = _bufs(c)
            xn_tm, xn_fm = _xn(c)
            M[f"ac_sgA{c}"] = cop("scalar",
                                  lambda e: e.activation(sc3[:, :], sc2[:, 0:1024], AF.Exp,
                                                         scale=np.float32(-1.0)),
                                  waits=[(DV, M[f"dv_ps{c}"])], inc=AC)
            M[f"ac_sgB{c}"] = cop("scalar",
                                  lambda e: e.activation(sc1[:, 1024:2048], sc3[:, :], AF.Ln,
                                                         bias=np.float32(1.0)),
                                  waits=[(AC, M[f"ac_sgA{c}"])], inc=AC)
            M[f"ac_sg{c}"] = cop("scalar",
                                 lambda e: e.activation(sc3[:, :], sc1[:, 1024:2048], AF.Exp,
                                                        scale=np.float32(-1.0)),
                                 waits=[(AC, M[f"ac_sgB{c}"])], inc=AC)

            def f_m3(e):
                r = None
                for k in range(16):
                    lhs = u_sb[:, k * 128:(k + 1) * 128]
                    for n in range(2):
                        r = e.matmul(ps_zd[:, n * 512:(n + 1) * 512], lhs,
                                     Wout_sb[:, k * 1024 + n * 512: k * 1024 + (n + 1) * 512],
                                     start=(k == 0), stop=False)
                for n in range(2):
                    r = e.matmul(ps_zd[:, n * 512:(n + 1) * 512], Ob_sb[:, :],
                                 bout_row[0:1, n * 512:(n + 1) * 512],
                                 start=False, stop=True)
                return r
            M[f"pe_m3{c}"] = cop("tensor", f_m3,
                                  waits=[(AC, M[f"ac_yev{c}"]), (AC, M[f"ac_cb{c}"]),
                                         (WO, M["dw_wout"])], inc=PE)


        def emit_tail_b(c):
            p, xb, ob = _bufs(c)
            xn_tm, xn_fm = _xn(c)
            M[f"dv_g{c}"] = cop("vector",
                                 lambda e: e.tensor_tensor(sc1[:, 1024:2048], sc2[:, 0:1024],
                                                           sc3[:, :], ALU.mult),
                                 waits=[(AC, M[f"ac_sg{c}"])], inc=DV)

            wl = [(PE, M[f"pe_m3{c}"]), (DV, M[f"dv_g{c}"])]
            if c == 0:
                wl.append((PE, M["pe_tokdone"]))
            M[f"dv_fa{c}"] = cop("vector",
                                 lambda e: e.scalar_tensor_tensor(sc3[:, :], ps_zd[:, :], 1.0,
                                                                  sc1[:, 1024:2048],
                                                                  op0=ALU.mult, op1=ALU.mult),
                                 waits=wl, inc=DV)
            wl2 = [(DV, M[f"dv_fa{c}"])]
            if c >= 2:
                wl2.append((SO[p], 16 * (c // 2)))
            v_final[c] = cop("vector",
                             (lambda e, xb=xb, ob=ob: e.tensor_tensor(ob[:, :], sc3[:, :],
                                                                      xb[:, :], ALU.add)),
                             waits=wl2, inc=DV)
            M[f"dv_final{c}"] = v_final[c]

            if not _trunc[0]:
                cop("sync", (lambda e, ob=ob, c=c: e.dma_start(
                    out=out_d[c * 128:(c + 1) * 128, :], in_=ob[:, :])),
                     waits=[(DV, v_final[c])], inc=SO[p], n=16)
                nso[p] += 1
                if c + 2 < nch:
                    x_load(c + 2, [(DV, v_final[c])])

        if debug_taps:
            def f_dbg_memset(e):
                e.memset(stat[:, :], 0.0)
                e.memset(w_t[:, :], 0.0)
                return e.memset(VW[:, :], 0.0)
            S.ops["vector"].insert(0, ([], f_dbg_memset, None, 1))
            taps = dict(t_stat=stat, t_xn_tm=xn_tm, t_xn_fm=xn_fm, t_u=u_sb,
                        t_xig=xi_g, t_xid=xi_dp, t_expz=expz, t_sums=sums,
                        t_mxs=mxs, t_m6s=m6s, t_m6=m6, t_eq6=eq6, t_ec6=ec6,
                        t_gates=gates, t_rsum=rsum, t_gsc=gsc, t_w=w_sb,
                        t_wt=w_t, t_ccol=carry_col, t_ct=carry_t, t_sc1=sc1,
                        t_sc2=sc2, t_sc3=sc3, t_gb=gb, t_dpb=dpb, t_VW=VW,
                        t_TG=TG, t_svec=scale_vec, t_wdwd=wdw_dmaj)
            allw = [(PE, S.val(PE)), (AC, S.val(AC)), (DV, S.val(DV))]
            for tn, th in taps.items():
                shp = list(th.shape)
                dt_ = th.dtype
                od = nc.dram_tensor(tn, shp, dt_, kind="ExternalOutput").ap()
                S.op("sync", (lambda e, od=od, th=th: e.dma_start(out=od, in_=th[:, :])),
                     waits=allw, inc=SO[0], n=16)
                nso[0] += 1


        emit_front(0)
        emit_front_b(0)
        for c in range(nch):
            emit_body(c)
            if c + 1 < nch:
                emit_front(c + 1)
            emit_body_c(c)
            if c + 1 < nch:
                emit_front_b(c + 1)
            emit_tail_a(c)
            emit_tail_b(c)

        fw = []
        if nso[0]:
            fw.append((SO[0], 16 * nso[0]))
        if nso[1]:
            fw.append((SO[1], 16 * nso[1]))
        if not fw:
            fw = [(DW, 16), (DC, 16)]
        S.op("sync", lambda e: e.nop(), waits=fw)

        with nc.Block() as block:
            def make_section(name):
                def body(eng):
                    if name == "scalar":
                        from concourse.hw_specs import get_activation_tables
                        for idx, funcs in enumerate(get_activation_tables(nc.m.arch).values()):
                            if AF.Ln in funcs and AF.Exp in funcs:
                                eng.add_instruction(mybir.InstLoadActFuncSet(
                                    name=nc.get_next_instruction_name(),
                                    act_func_set_id=idx, ins=[], outs=[]))
                                break
                    for waits, fn, inc, n in S.ops[name]:
                        for smp, v in waits:
                            eng.wait_ge(smp, v)
                        inst = fn(eng)
                        if inc is not None:
                            inst.then_inc(inc, n)
                return body
            block.sync(make_section("sync"))
            block.gpsimd(make_section("gpsimd"))
            block.tensor(make_section("tensor"))
            block.scalar(make_section("scalar"))
            block.vector(make_section("vector"))

    return nc


_CONSTS = _host_consts()
_PROG = None


def kernel(L0_attn_x, L0_attn_Win, L0_attn_bin_, L0_attn_Bp, L0_attn_Cp,
           L0_attn_Dp, L0_attn_Wout, L0_attn_bout, L0_ffn_A, L0_ffn_B,
           L0_ffn_w_gate, L0_ffn_wt_w, L0_ffn_wt_b, L0_ffn_wd_w, L0_ffn_wd_b,
           L0_ffn_scale):
    global _PROG
    if _PROG is None:
        _PROG = build_program()
    nc = _PROG

    f = np.asarray
    shared = {
        "Win": f(L0_attn_Win, dtype=np.float32),
        "bin_": f(L0_attn_bin_, dtype=np.float32).reshape(1, DI),
        "Bp": f(L0_attn_Bp, dtype=np.float32),
        "Cp": f(L0_attn_Cp, dtype=np.float32),
        "Dp": f(L0_attn_Dp, dtype=np.float32).reshape(1, DI),
        "Wout": f(L0_attn_Wout, dtype=np.float32),
        "bout": f(L0_attn_bout, dtype=np.float32).reshape(1, D),
        "A": f(L0_ffn_A, dtype=np.float32),
        "B": f(L0_ffn_B, dtype=np.float32),
        "w_gate": f(L0_ffn_w_gate, dtype=np.float32),
        "wt_w": f(L0_ffn_wt_w, dtype=np.float32),
        "wt_b": f(L0_ffn_wt_b, dtype=np.float32).reshape(8, 128),
        "wd_w": f(L0_ffn_wd_w, dtype=np.float32),
        "wd_b": f(L0_ffn_wd_b, dtype=np.float32).reshape(1, D),
        "scale": f(L0_ffn_scale, dtype=np.float32).reshape(1, 1),
    }
    shared.update(_CONSTS)
    x = np.ascontiguousarray(f(L0_attn_x, dtype=np.float32))
    in_maps = []
    for i in range(NCORE):
        m = dict(shared)
        m["x"] = np.ascontiguousarray(x[i])
        in_maps.append(m)

    res = run_bass_kernel_spmd(nc, in_maps, core_ids=list(range(NCORE)))
    out = np.stack([np.asarray(r["out"], dtype=np.float32) for r in res.results], axis=0)
    return out
